# version 20
# baseline (speedup 1.0000x reference)
import numpy as np
import ml_dtypes

import concourse.bass as bass
import concourse.mybir as mybir
from concourse.bass_utils import run_bass_kernel_spmd

F32 = mybir.dt.float32
BF16 = mybir.dt.bfloat16
AF = mybir.ActivationFunctionType
ALU = mybir.AluOpType
AX = mybir.AxisListType.X

D = 2048
NC16 = 16
H = 16
HD = 128
FF = 8192
FT = FF // 128
GSZ = 4
EPS = 1e-5

COMPUTE_SEMS = ("pe", "act", "dve")
BODY_OF = {"pe": "tensor", "act": "scalar", "sc": "scalar", "dve": "vector",
           "sy": "sync", "gp": "gpsimd"}


def _flat(waits):
    out = []
    for w in waits:
        if w is None:
            continue
        if isinstance(w, list):
            out.extend(x for x in w if x is not None)
        else:
            out.append(w)
    return out


class Plan:
    def __init__(self):
        self.steps = []
        self.counts = {}

    def add(self, engine, fn, waits=(), inc=None, amt=1):
        if inc is None and engine in ("act", "dve"):
            inc = engine
        tok = None
        snap = dict(self.counts)
        if inc is not None:
            c = self.counts.get(inc, 0) + amt
            self.counts[inc] = c
            tok = (inc, c, engine)
        self.steps.append((engine, fn, _flat(waits), inc, amt, snap))
        return tok

    def wait_only(self, engine, waits):
        self.steps.append((engine, None, _flat(waits), None, 0, dict(self.counts)))


def emit(plan, eng, names, sems, serialize=False):
    last_wait = {}
    for engine, fn, waits, inc, amt, snap in plan.steps:
        if engine not in names:
            continue
        if serialize:
            waits = [(s, v, "__ext__") for s, v in snap.items()]
        elif engine in ("act", "dve"):
            waits = waits + [(engine, snap.get(engine, 0), "__self__")]
        for (sem, val, prod) in waits:
            if not serialize and sem == "pe" and engine == "pe":
                continue
            if last_wait.get(sem, -1) >= val:
                continue
            eng.wait_ge(sems[sem], val)
            last_wait[sem] = val
        if fn is None:
            continue
        inst = fn(eng)
        if inc is not None:
            inst.then_inc(sems[inc], amt)


def build(T=4096, C=1024, dbg=False, serialize=False):
    assert C == 1024 and T % C == 0
    NCH = T // C
    NTK = T // 128
    nc = bass.Bass("TRN2", target_bir_lowering=False)

    xT = nc.dram_tensor("xT", [D, T], F32, kind="ExternalInput")
    watt = nc.dram_tensor("watt", [D, 2304], F32, kind="ExternalInput")
    wout = nc.dram_tensor("wout", [D, D], F32, kind="ExternalInput")
    wgT = nc.dram_tensor("wgT", [D, FF], F32, kind="ExternalInput")
    wuT = nc.dram_tensor("wuT", [D, FF], F32, kind="ExternalInput")
    wdT = nc.dram_tensor("wdT", [FF, D], F32, kind="ExternalInput")
    tabC = nc.dram_tensor("tabC", [128, T], BF16, kind="ExternalInput")
    tabS = nc.dram_tensor("tabS", [128, T], BF16, kind="ExternalInput")
    identD = nc.dram_tensor("identD", [128, 128], BF16, kind="ExternalInput")
    onesD = nc.dram_tensor("onesD", [128, 128], BF16, kind="ExternalInput")
    zT = nc.dram_tensor("zT", [D, T], F32, kind="ExternalOutput")
    woutB = nc.dram_tensor("woutB", [NC16, 128, D], BF16, kind="Internal")

    xT_r = xT.ap().rearrange("(c p) t -> p c t", p=128)
    watt_r = watt.ap().rearrange("(c p) o -> p c o", p=128)
    wout_r = wout.ap().rearrange("(c p) o -> p c o", p=128)

    wgT_r = wgT.ap().rearrange("(c p) f -> p c f", p=128)
    wuT_r = wuT.ap().rearrange("(c p) f -> p c f", p=128)
    wdT_r = wdT.ap().rearrange("(t p) e -> p t e", p=128)
    zT_r = zT.ap().rearrange("(c p) t -> p c t", p=128)

    dbg_outs = {}
    if dbg:
        dbg_outs["dKT"] = nc.dram_tensor("dKT", [128, T], BF16, kind="ExternalOutput")
        dbg_outs["dVtok"] = nc.dram_tensor("dVtok", [128, NTK * 128], BF16, kind="ExternalOutput")
        dbg_outs["dkmax"] = nc.dram_tensor("dkmax", [128, 1], F32, kind="ExternalOutput")
        dbg_outs["daoT"] = nc.dram_tensor("daoT", [128, NC16 * C], BF16, kind="ExternalOutput")
        dbg_outs["dy"] = nc.dram_tensor("dy", [128, NC16 * C], F32, kind="ExternalOutput")
        dbg_outs["dr2"] = nc.dram_tensor("dr2", [128, NC16 * C], F32, kind="ExternalOutput")

    def sb(name, shape, dtype):
        return nc.alloc_sbuf_tensor(name, shape, dtype)

    KT = sb("KT", [128, T], BF16)
    Vtok = sb("Vtok", [128, NTK, 128], BF16)
    ones_sb = sb("ones_sb", [128, 128], BF16)
    kmax2 = sb("kmax2", [128, 1], F32)
    knt = sb("knt", [128, 1], F32)
    qm = sb("qm", [128, 1], F32)
    negM = [sb(f"negM{i}", [128, 1], F32) for i in range(3)]
    ybf = sb("ybf", [128, NC16, C], BF16)
    acc = sb("acc", [128, NC16, C], F32)
    xbf = sb("xbf", [128, NC16, C], BF16)
    wqh = [sb(f"wqh{i}", [128, NC16, 128], BF16) for i in range(2)]
    tabQc = sb("tabQc", [128, C], BF16)
    tabQs = sb("tabQs", [128, C], BF16)

    arena = nc.sbuf_base

    wkv = sb("wkv", [128, NC16, 256], BF16)
    ident = sb("ident", [128, 128], BF16)
    tkc = [sb("tkc0", [128, 512], BF16)]
    tks = [sb("tks0", [128, 512], BF16)]
    rotb1 = sb("rotb1", [128, 512], F32)
    t1b1 = sb("t1b1", [128, 512], F32)
    t2b1 = sb("t2b1", [128, 512], F32)
    vt1 = sb("vt1", [128, 512], BF16)
    ksq1 = sb("ksq1", [128, 512], BF16)
    xbf_p = sb("xbf_p", [128, NC16, 512], BF16)
    p1_end = nc.sbuf_base

    nc.sbuf_base = arena
    aoT = sb("aoT", [128, NC16, C], BF16)
    heads_base = nc.sbuf_base
    P_ring = sb("P_ring", [128, 4, 512], BF16)
    Qp = [sb(f"Qp{i}", [128, C], BF16) for i in range(3)]
    rotb = sb("rotb", [128, 512], BF16)
    t1b = sb("t1b", [128, 512], F32)
    t2b = sb("t2b", [128, 512], BF16)
    qsq = sb("qsq", [128, C], BF16)
    rsum = sb("rsum", [128, 512], F32)
    heads_end = nc.sbuf_base
    nc.sbuf_base = heads_base
    xf = [sb(f"xf{i}", [128, 512], F32) for i in range(2)]
    lnA = sb("lnA", [128, 256], F32)
    lnB = sb("lnB", [128, 256], F32)
    lnC = sb("lnC", [128, 256], F32)
    lnD = sb("lnD", [128, 256], F32)
    rbf = [sb(f"rbf{i}", [128, 4, 256], BF16) for i in range(2)]
    rsq = [sb(f"rsq{i}", [128, 4, 256], BF16) for i in range(2)]
    lnU = sb("lnU", [128, 2, 256], F32)
    ln1_end = nc.sbuf_base
    attn_end = max(heads_end, ln1_end)

    nc.sbuf_base = arena
    wg = [sb(f"wg{i}", [128, NC16, 128], BF16) for i in range(2)]
    wu = [sb(f"wu{i}", [128, NC16, 128], BF16) for i in range(2)]
    wdg = sb("wdg", [128, GSZ, D], BF16)
    gbuf = sb("gbuf", [128, GSZ, C], BF16)
    silu_sb = sb("silu_sb", [128, C], F32)
    ffn_end = nc.sbuf_base
    nc.sbuf_base = arena
    z_stage = sb("z_stage", [128, NC16, 256], F32)
    lnA2 = sb("lnA2", [128, 256], F32)
    lnB2 = sb("lnB2", [128, 256], F32)
    lnC2 = sb("lnC2", [128, 256], F32)
    lnD2 = sb("lnD2", [128, 256], F32)
    rbf2 = [sb(f"rbf2{i}", [128, 4, 256], BF16) for i in range(2)]
    rsq2 = [sb(f"rsq2{i}", [128, 4, 256], BF16) for i in range(2)]
    lnU2 = sb("lnU2", [128, 2, 256], F32)
    ln2_end = nc.sbuf_base

    hi = max(p1_end, attn_end, ffn_end, ln2_end)
    assert hi <= nc.sbuf_top, f"SBUF over budget: {hi} > {nc.sbuf_top}"
    nc.sbuf_base = hi

    pb0 = nc.psum_base
    _ = nc.alloc_psum_tensor("ps_gap", [128, 1024], F32)
    psK = [nc.alloc_psum_tensor(f"psK{i}", [128, 512], F32) for i in range(2)]
    psV = [nc.alloc_psum_tensor(f"psV{i}", [128, 512], F32) for i in range(2)]
    psTR = nc.alloc_psum_tensor("psTR", [128, 512], F32)
    psKN = nc.alloc_psum_tensor("psKN", [128, 512], F32)
    nc.psum_base = pb0
    ring = [nc.alloc_psum_tensor(f"ring{i}", [128, 1024], F32) for i in range(3)]
    accS = nc.alloc_psum_tensor("accS", [128, 512], F32)
    accA = nc.alloc_psum_tensor("accA", [128, 512], F32)
    nc.psum_base = pb0
    psD = [nc.alloc_psum_tensor(f"psD{i}", [128, 1024], F32) for i in range(2)]
    psG = nc.alloc_psum_tensor("psG", [128, 1024], F32)
    psU = nc.alloc_psum_tensor("psU", [128, 1024], F32)
    assert nc.psum_base - pb0 <= 8

    plan = Plan()
    add = plan.add
    ndbg = [0]

    def dbg_sem():
        ndbg[0] += 1
        return f"d_dbg{ndbg[0]}"

    ring_tok = [None, None, None]
    ring_i = [0]

    def ring_acquire():
        idx = ring_i[0] % 3
        ring_i[0] += 1
        return idx, ring_tok[idx]

    ppair_tok = [None, None]
    accS_tok = [None]
    accA_tok = [None]

    t_ones = add("sy", lambda e: e.dma_start(ones_sb[:, :], onesD.ap()), inc="d_ones", amt=16)
    t_ident = add("sy", lambda e: e.dma_start(ident[:, :], identD.ap()), inc="d_ident", amt=16)
    t_wkv = add("gp", lambda e: e.dma_start(wkv[:, :, :], watt_r[:, :, 2048:2304]),
                inc="d_wkv", amt=16)
    t_xbf = add("gp", lambda e: e.dma_start(xbf[:, :, :], xT_r[:, :, 0:C]),
                inc="d_xbf", amt=16)
    t_tabQ1 = add("sy", lambda e: e.dma_start(tabQc[:, :], tabC.ap()[:, 0:C]),
                  inc="d_tabqc", amt=16)
    t_tabQ2 = add("sy", lambda e: e.dma_start(tabQs[:, :], tabS.ap()[:, 0:C]),
                  inc="d_tabqs", amt=16)
    t_wqh = [add("gp", lambda e, i=i: e.dma_start(wqh[i][:, :, :],
                                                  watt_r[:, :, i * 128:(i + 1) * 128]),
                 inc=f"d_wqh{i}", amt=16) for i in range(2)]

    wqh_tok = [None, None]
    negM_rd = [None, None, None]
    qp_rd = [None, None, None]
    qsq_rd = [None]
    xbf_rd = [None]
    ybf_rd = [None]
    aoT_rd = [None]
    tabQ_rd = [None]
    g_rd = [None]
    wdg_rd = [None]
    acc_zrd = [None]
    zdma_tok = [None]
    psD_tok = [None, None]
    psG_tok = [None, None]
    psU_tok = [None, None]
    rope_rd = [None]
    ln_state1 = {"prev_rstd": None, "prev_apply": None}
    ln_state2 = {"prev_rstd": None, "prev_apply": None}
    qp_toks = {}
    negM_tok = {}
    last_ao = [None]
    last_evac = [None]
    ybf_ready = [None]
    silu_tok = [None, None]
    gmul_tok = [None, None]

    def partA(h, cc):
        wb = h % 2
        qb = h % 3
        ridx, rtok = ring_acquire()

        def qfill(e, r=ridx, b=wb):
            for s2 in range(2):
                for c in range(NC16):
                    i = e.matmul(ring[r][:, s2 * 512:(s2 + 1) * 512],
                                 wqh[b][:, c, :], xbf[:, c, s2 * 512:(s2 + 1) * 512],
                                 start=(c == 0), stop=(c == NC16 - 1))
            return i
        t_qf = add("pe", qfill, waits=[t_wqh[wb], t_xbf, rtok], inc="pe")
        wqh_tok[wb] = t_qf
        xbf_rd[0] = t_qf
        if h + 2 < H:
            t_wqh[(h + 2) % 2] = add(
                "gp", lambda e, hh=h + 2: e.dma_start(
                    wqh[hh % 2][:, :, :], watt_r[:, :, hh * 128:(hh + 1) * 128]),
                waits=[wqh_tok[(h + 2) % 2]], inc=f"d_wqh{(h + 2) % 2}", amt=16)

        toks = []
        last_t1 = None
        for s2 in range(2):
            sc_ = slice(s2 * 512, (s2 + 1) * 512)
            t_cp1 = add("dve", lambda e, r=ridx, s=sc_: e.tensor_copy(rotb[0:64, :],
                                                                      ring[r][64:128, s]),
                        waits=[t_qf, rope_rd[0]])
            t_cp2 = add("dve", lambda e, r=ridx, s=sc_: e.tensor_copy(rotb[64:128, :],
                                                                      ring[r][0:64, s]))
            t_t1 = add("dve", lambda e, r=ridx, s=sc_: e.tensor_mul(
                t1b[:, :], ring[r][:, s], tabQc[:, s]),
                waits=[t_qf, t_tabQ1, rope_rd[0]])
            t_t2 = add("dve", lambda e, s=sc_: e.tensor_mul(t2b[:, :], rotb[:, :],
                                                            tabQs[:, s]),
                       waits=[t_tabQ2])
            t_qp = add("dve", lambda e, b=qb, s=sc_: e.tensor_add(Qp[b][:, s], t1b[:, :],
                                                                  t2b[:, :]),
                       waits=[qp_rd[qb]])
            toks.append(t_qp)
            rope_rd[0] = t_qp
            last_t1 = t_t1
        ring_tok[ridx] = last_t1
        tabQ_rd[0] = toks[-1]
        qp_toks[h] = toks

    def partA_sq(h):
        hb = h % 3
        t_sq = add("act", lambda e, b=hb: e.activation(qsq[:, :], Qp[b][:, :], AF.Square),
                   waits=qp_toks[h][:2] + [qsq_rd[0]])
        qp_toks[h].append(t_sq)

    def partB(h):
        hb = h % 3
        r2idx, r2tok = ring_acquire()
        t_sq = qp_toks[h][2]

        def qnfill(e, r=r2idx):
            e.matmul(ring[r][:, 0:512], ones_sb[:, :], qsq[:, 0:512], start=True, stop=True)
            return e.matmul(ring[r][:, 512:1024], ones_sb[:, :], qsq[:, 512:1024],
                            start=True, stop=True)
        t_qn = add("pe", qnfill, waits=[t_sq, t_ones, r2tok], inc="pe")
        qsq_rd[0] = t_qn
        t_qm = add("dve", lambda e, r=r2idx: e.reduce_max(qm[:, :], ring[r][:, :], axis=AX),
                   waits=[t_qn])
        ring_tok[r2idx] = t_qm
        negM_tok[h] = add("dve", lambda e, b=hb: e.tensor_scalar(
            negM[b][:, :], qm[:, :], kmax2[:, :], -0.5, ALU.add, ALU.mult),
            waits=[negM_rd[hb], kmax_tok])

    def attention(h, s2):
        hb = h % 3
        if True:
            scq = slice(s2 * 512, (s2 + 1) * 512)
            t_sum = t_av = None
            exp_q = []

            def emit_sumav(jp, t_exp):
                nonlocal t_sum, t_av

                def savf(e, q=2 * (jp % 2), a=2 * jp, b=2 * jp + 1,
                         first=(jp == 0), last=(jp == NTK // 2 - 1)):
                    e.matmul(accA[:, :], Vtok[:, a, :], P_ring[:, q, :],
                             start=first, stop=False)
                    e.matmul(accA[:, :], Vtok[:, b, :], P_ring[:, q + 1, :],
                             start=False, stop=last)
                    e.matmul(accS[:, :], ones_sb[:, :], P_ring[:, q, :],
                             start=first, stop=False)
                    return e.matmul(accS[:, :], ones_sb[:, :], P_ring[:, q + 1, :],
                                    start=False, stop=last)
                t_av = add("pe", savf,
                           waits=[t_exp, vtok_done,
                                  accS_tok[0] if jp == 0 else None,
                                  accA_tok[0] if jp == 0 else None], inc="pe")
                t_sum = t_av
                ppair_tok[jp % 2] = t_av

            for jp in range(NTK // 2):
                sidx, stok = ring_acquire()
                j0, j1 = 2 * jp, 2 * jp + 1

                def sfill(e, r=sidx, a=j0, b=j1, bb=hb, s=scq):
                    e.matmul(ring[r][:, 0:512], KT[:, a * 128:(a + 1) * 128], Qp[bb][:, s],
                             start=True, stop=True)
                    return e.matmul(ring[r][:, 512:1024], KT[:, b * 128:(b + 1) * 128],
                                    Qp[bb][:, s], start=True, stop=True)
                t_S = add("pe", sfill, waits=[stok, qp_toks[h][s2], kt_done], inc="pe")
                qp_rd[hb] = t_S

                ps = 2 * (jp % 2)
                t_exp = add("act", lambda e, r=sidx, q=ps, b=hb: e.activation(
                    P_ring[:, q:q + 2, :],
                    ring[r][:, :].rearrange("p (j t) -> p j t", j=2),
                    AF.Exp, bias=negM[b][:, :]),
                    waits=[t_S, negM_tok[h], ppair_tok[jp % 2]])
                ring_tok[sidx] = t_exp
                negM_rd[hb] = t_exp
                exp_q.append(t_exp)
                if jp >= 1:
                    emit_sumav(jp - 1, exp_q[jp - 1])
            emit_sumav(NTK // 2 - 1, exp_q[-1])

            t_rs = add("dve", lambda e: e.reciprocal(rsum[:, :], accS[:, :]),
                       waits=[t_sum])
            accS_tok[0] = t_rs
            t_ao = add("dve", lambda e, hh=h, s=scq: e.tensor_mul(
                aoT[:, hh, s], accA[:, :], rsum[:, :]),
                waits=[t_av, aoT_rd[0]])
            accA_tok[0] = t_ao
            last_ao[0] = t_ao

    def bc2(t):
        return bass.AP(t, 0, [[256, 128], [0, 2], [1, 256]])


    NP1 = T // 512
    xp_tok = None
    xp_rd = [None]
    tk_tok = [None, None]
    tk_rd = [None, None]
    psK_rd = [None, None]
    psV_rd = [None, None]
    psTR_rd = None
    psKN_rd = None
    rope1_rd = None
    vt_rd = None
    ksq_rd = None
    kt_done = None
    vtok_done = None
    kmax_tok = None

    prologue0_hook = [lambda: (partA(0, 0), partA_sq(0))]
    for p in range(NP1):
        if p == min(4, NP1 - 1) and prologue0_hook:
            prologue0_hook.pop()()
        pbuf = p % 2
        cols = slice(p * 512, (p + 1) * 512)
        xp_tok = add("gp", lambda e, c=cols: e.dma_start(
            xbf_p[:, :, :], xT_r[:, :, c]),
            waits=[xp_rd[0]], inc="d_xbfp", amt=16)
        tk_tok[0] = add("sy", lambda e, c=cols: e.dma_start(
            tkc[0][:, :], tabC.ap()[:, c]), waits=[tk_rd[0]], inc="d_tkc0", amt=16)
        tk_tok2 = add("sy", lambda e, c=cols: e.dma_start(
            tks[0][:, :], tabS.ap()[:, c]), waits=[tk_rd[0]], inc="d_tks0", amt=16)

        def kfill(e, b=pbuf):
            for c in range(NC16):
                i = e.matmul(psK[b][:, :], wkv[:, c, 0:128], xbf_p[:, c, :],
                             start=(c == 0), stop=(c == NC16 - 1))
            return i
        t_kf = add("pe", kfill, waits=[xp_tok, t_wkv, psK_rd[pbuf]], inc="pe")

        def vfill(e, b=pbuf):
            for c in range(NC16):
                i = e.matmul(psV[b][:, :], wkv[:, c, 128:256], xbf_p[:, c, :],
                             start=(c == 0), stop=(c == NC16 - 1))
            return i
        t_vf = add("pe", vfill, waits=[xp_tok, t_wkv, psV_rd[pbuf]], inc="pe")
        xp_rd[0] = t_vf

        t_cp1 = add("act", lambda e, b=pbuf: e.copy(rotb1[0:64, :], psK[b][64:128, :]),
                    waits=[t_kf, rope1_rd])
        t_cp2 = add("act", lambda e, b=pbuf: e.copy(rotb1[64:128, :], psK[b][0:64, :]))
        t_t1 = add("dve", lambda e, b=pbuf: e.tensor_mul(t1b1[:, :], psK[b][:, :],
                                                         tkc[0][:, :]),
                   waits=[t_kf, t_cp2, tk_tok[0], rope1_rd])
        psK_rd[pbuf] = [t_t1, t_cp2]
        t_t2 = add("dve", lambda e: e.tensor_mul(t2b1[:, :], rotb1[:, :],
                                                 tks[0][:, :]),
                   waits=[tk_tok2])
        tk_rd[0] = t_t2
        t_kt = add("dve", lambda e, c=cols: e.tensor_add(KT[:, c], t1b1[:, :], t2b1[:, :]))
        rope1_rd = t_kt
        kt_done = t_kt

        t_vt = add("act", lambda e, b=pbuf: e.copy(vt1[:, :], psV[b][:, :]),
                   waits=[t_vf, vt_rd])
        psV_rd[pbuf] = t_vt

        def trfill(e):
            for q in range(4):
                i = e.matmul(psTR[:, q * 128:(q + 1) * 128], vt1[:, q * 128:(q + 1) * 128],
                             ident[:, :], start=True, stop=True)
            return i
        t_tr = add("pe", trfill, waits=[t_vt, t_ident, psTR_rd], inc="pe")
        vt_rd = t_tr
        t_ev = add("act", lambda e, q=p: e.copy(
            Vtok[:, q * 4:(q + 1) * 4, :],
            psTR[:, :].rearrange("p (j h) -> p j h", j=4)), waits=[t_tr])
        psTR_rd = t_ev
        vtok_done = t_ev

        t_ks = add("act", lambda e, c=cols: e.activation(ksq1[:, :], KT[:, c], AF.Square),
                   waits=[t_kt, ksq_rd])
        t_kn = add("pe", lambda e: e.matmul(psKN[:, :], ones_sb[:, :], ksq1[:, :],
                                            start=True, stop=True),
                   waits=[t_ks, t_ones, psKN_rd], inc="pe")
        ksq_rd = t_kn
        t_red = add("dve", lambda e: e.reduce_max(knt[:, :], psKN[:, :], axis=AX),
                    waits=[t_kn])
        psKN_rd = t_red
        if p == 0:
            kmax_tok = add("dve", lambda e: e.tensor_copy(kmax2[:, :], knt[:, :]))
        else:
            kmax_tok = add("dve", lambda e: e.tensor_max(kmax2[:, :], kmax2[:, :], knt[:, :]))

    wo_prep = None
    for q in range(NC16):
        wo_prep = add("gp", lambda e, q=q: e.dma_start(
            woutB.ap()[q].rearrange("p (c o) -> p c o", c=NC16),
            wout_r[:, :, q * 128:(q + 1) * 128]), inc="d_woprep", amt=16)

    ring_tok[1] = _flat([ring_tok[1], psK_rd[0], psK_rd[1]])
    ring_tok[2] = _flat([ring_tok[2], [psV_rd[0], psV_rd[1]]])
    accS_tok[0] = _flat([accS_tok[0], [psTR_rd]])
    accA_tok[0] = _flat([accA_tok[0], [psKN_rd]])

    if dbg:
        add("sc", lambda e: e.dma_start(dbg_outs["dKT"].ap(), KT[:, :]),
            waits=[kt_done], inc=dbg_sem(), amt=16)
        add("sc", lambda e: e.dma_start(
            dbg_outs["dVtok"].ap(), Vtok[:, :, :].rearrange("p j h -> p (j h)")),
            waits=[vtok_done], inc=dbg_sem(), amt=16)
        add("sc", lambda e: e.dma_start(dbg_outs["dkmax"].ap(), kmax2[:, :]),
            waits=[kmax_tok], inc=dbg_sem(), amt=16)

    def emit_ln(src4, apply_out4, src2, apply_out2, rbf_, rsq_, A, B, Cc, Dd, U,
                stats_waits, apply_waits, extra4, state):
        rb_rd = [None, None]
        t_mm = None
        for g in range(4):
            gb = g % 2
            t_rb = add("act", lambda e, g_=g, b=gb: e.copy(rbf_[b][:, :, :], src4(g_)),
                       waits=stats_waits + [rb_rd[gb]])
            t_rq = add("dve", lambda e, b=gb: e.tensor_mul(rsq_[b][:, :, :],
                                                           rbf_[b][:, :, :],
                                                           rbf_[b][:, :, :]),
                       waits=[t_rb])

            def statmm(e, b=gb, first=(g == 0), last=(g == 3)):
                for j in range(4):
                    e.matmul(accS[:, 0:256], ones_sb[:, :], rbf_[b][:, j, :],
                             start=(first and j == 0), stop=(last and j == 3))
                    i = e.matmul(accA[:, 0:256], ones_sb[:, :], rsq_[b][:, j, :],
                                 start=(first and j == 0), stop=(last and j == 3))
                return i
            t_mm = add("pe", statmm,
                       waits=[t_rb, t_rq, accS_tok[0] if g == 0 else None,
                              accA_tok[0] if g == 0 else None], inc="pe")
            rb_rd[gb] = t_mm
        t_A = add("dve", lambda e: e.tensor_scalar_mul(A[:, :], accS[:, 0:256], 1.0 / D),
                  waits=[t_mm, state["prev_rstd"]])
        t_B = add("dve", lambda e: e.tensor_scalar_mul(B[:, :], accA[:, 0:256], 1.0 / D))
        accS_tok[0] = t_A
        accA_tok[0] = t_B
        add("dve", lambda e: e.tensor_mul(Cc[:, :], A[:, :], A[:, :]))
        t_var = add("dve", lambda e: e.scalar_tensor_tensor(
            Dd[:, :], B[:, :], float(EPS), Cc[:, :], ALU.add, ALU.subtract))
        t_ln = add("act", lambda e: e.activation(B[:, :], Dd[:, :], AF.Ln),
                   waits=[t_var])
        t_rstd = add("act", lambda e: e.activation(Cc[:, :], B[:, :], AF.Exp, scale=-0.5))
        state["prev_rstd"] = t_rstd
        add("dve", lambda e: e.tensor_mul(Dd[:, :], A[:, :], Cc[:, :]), waits=[t_rstd])
        out_toks = []
        for g in range(8):
            add("dve", lambda e, g_=g: e.tensor_mul(U[:, :, :], src2(g_), bc2(Cc)),
                waits=[t_rstd] + apply_waits)
            t_y = add("dve", lambda e, g_=g: e.tensor_sub(apply_out2(g_), U[:, :, :],
                                                          bc2(Dd)))
            if g % 2 == 1:
                ee = extra4(g // 2, t_y)
                out_toks.append(ee if ee is not None else t_y)
        return out_toks

    n_out = 0
    prologue_state = [1, 0]
    for cc in range(NCH):
        pa, pb = prologue_state
        for h in range(H):
            if h == 0 and h >= pb:
                partB(h)
            if h + 1 < H and h + 1 >= pa:
                partA(h + 1, cc)
            attention(h, 0)
            if h + 1 < H and h + 1 >= pa:
                partA_sq(h + 1)
            if h + 1 < H and h + 1 >= pb:
                partB(h + 1)
            attention(h, 1)

        wo_tok = [None, None]
        xf_tok = [None, None]
        xf_rd = [last_ao[0], last_ao[0]]
        last_pj = [None]
        ybf_half = [None, None]
        evac_of_piece = [None, None]

        def proj_piece(piece):
            pc = slice(piece * 512, (piece + 1) * 512)
            gpc0 = cc * C + piece * 512
            for e_t in range(NC16):
                eb = e_t % 2
                wo_tok[eb] = add("sy", lambda e, et=e_t: e.dma_start(
                    wqh[et % 2][:, :, :],
                    woutB.ap()[et].rearrange("p (c o) -> p c o", c=NC16)),
                    waits=[wqh_tok[eb], wo_prep], inc=f"d_wo{eb}", amt=16)
                xb = e_t % 2
                xf_tok[xb] = add("sc", lambda e, et=e_t, g=slice(gpc0, gpc0 + 512),
                                 b=xb: e.dma_start(
                    xf[b][:, :], xT_r[:, et, g]), waits=[xf_rd[xb]], inc=f"d_xf{xb}", amt=16)
                ridx, rtok = ring_acquire()

                def pjfill(e, r=ridx, b=eb, p=pc):
                    for c in range(NC16):
                        i = e.matmul(ring[r][:, 0:512], wqh[b][:, c, :], aoT[:, c, p],
                                     start=(c == 0), stop=(c == NC16 - 1))
                    return i
                t_pj = add("pe", pjfill, waits=[wo_tok[eb], last_ao[0], rtok], inc="pe")
                wqh_tok[eb] = t_pj
                aoT_rd[0] = t_pj
                last_pj[0] = t_pj
                t_ev = add("dve", lambda e, r=ridx, et=e_t, p=pc, b=xb: e.tensor_add(
                    acc[:, et, p], ring[r][:, 0:512], xf[b][:, :]),
                    waits=[t_pj, xf_tok[xb], acc_zrd[0]])
                ring_tok[ridx] = t_ev
                xf_rd[xb] = t_ev
                evac_of_piece[piece] = t_ev

        def ln1_sub(piece, sub):
            ppc = slice(piece * 512 + sub * 256, piece * 512 + sub * 256 + 256)

            def extra1(g_, t_y, s=ppc):
                return add("act", lambda e, g2=g_, s=s: e.copy(
                    ybf[:, 4 * g2:4 * g2 + 4, s], acc[:, 4 * g2:4 * g2 + 4, s]),
                    waits=[t_y, ybf_rd[0]])
            toks = emit_ln(
                lambda g_, s=ppc: acc[:, 4 * g_:4 * g_ + 4, s],
                lambda g_, s=ppc: acc[:, 4 * g_:4 * g_ + 4, s],
                lambda g_, s=ppc: acc[:, 2 * g_:2 * g_ + 2, s],
                lambda g_, s=ppc: acc[:, 2 * g_:2 * g_ + 2, s],
                rbf, rsq, lnA, lnB, lnC, lnD, lnU, [evac_of_piece[piece]], [],
                extra1, ln_state1)
            return toks[-1]

        proj_piece(0)
        ln1_sub(0, 0)
        proj_piece(1)
        ybf_half[0] = ln1_sub(0, 1)
        ln1_sub(1, 0)
        ybf_half[1] = ln1_sub(1, 1)
        ybf_ready[0] = ybf_half[1]

        dbg_ao_tok = None
        if dbg and cc == 0:
            dbg_ao_tok = add("sc", lambda e: e.dma_start(
                dbg_outs["daoT"].ap(), aoT[:, :, :].rearrange("p c t -> p (c t)")),
                waits=[aoT_rd[0], last_ao[0]], inc=dbg_sem(), amt=16)
            dbg_y_tok = add("sc", lambda e: e.dma_start(
                dbg_outs["dy"].ap(), acc[:, :, :].rearrange("p c t -> p (c t)")),
                waits=[ybf_ready[0]], inc=dbg_sem(), amt=16)
            ybf_ready[0] = [ybf_ready[0], dbg_y_tok]

        if cc + 1 < NCH:
            nch = slice((cc + 1) * C, (cc + 2) * C)
            t_xbf = add("gp", lambda e, s=nch: e.dma_start(xbf[:, :, :], xT_r[:, :, s]),
                        waits=[xbf_rd[0]], inc="d_xbf", amt=16)
            t_tabQ1 = add("sy", lambda e, s=nch: e.dma_start(tabQc[:, :], tabC.ap()[:, s]),
                          waits=[tabQ_rd[0]], inc="d_tabqc", amt=16)
            t_tabQ2 = add("sy", lambda e, s=nch: e.dma_start(tabQs[:, :], tabS.ap()[:, s]),
                          waits=[tabQ_rd[0]], inc="d_tabqs", amt=16)
            for i in range(2):
                t_wqh[i] = add("gp", lambda e, i=i: e.dma_start(
                    wqh[i][:, :, :], watt_r[:, :, i * 128:(i + 1) * 128]),
                    waits=[wqh_tok[i]], inc=f"d_wqh{i}", amt=16)

        ffn_war = [last_pj[0], zdma_tok[0], acc_zrd[0], dbg_ao_tok]
        psG_tok[0] = _flat([psG_tok[0], ring_tok[2]])
        psG_tok[1] = _flat([psG_tok[1], ring_tok[2]])
        psU_tok[0] = _flat([psU_tok[0], [accS_tok[0], accA_tok[0]]])
        psU_tok[1] = _flat([psU_tok[1], [accS_tok[0], accA_tok[0]]])
        psD_tok[0] = _flat([psD_tok[0], ring_tok[0]])
        psD_tok[1] = _flat([psD_tok[1], ring_tok[1]])
        wg_tok = [None, None]
        wu_tok = [None, None]
        wg_rd = [ffn_war, ffn_war]
        wu_rd = [ffn_war, ffn_war]
        wdg_rd[0] = _flat([wdg_rd[0], ffn_war])
        for fg in range(FT // GSZ):
            wd_toks = []
            for f4 in range(GSZ):
                ftg = fg * GSZ + f4
                fb = ftg % 2
                wg_tok[fb] = add("gp", lambda e, f=ftg: e.dma_start(
                    wg[f % 2][:, :, :], wgT_r[:, :, f * 128:(f + 1) * 128]),
                    waits=[wg_rd[fb]], inc=f"d_wg{fb}", amt=16)
                wu_tok[fb] = add("gp", lambda e, f=ftg: e.dma_start(
                    wu[f % 2][:, :, :], wuT_r[:, :, f * 128:(f + 1) * 128]),
                    waits=[wu_rd[fb]], inc=f"d_wu{fb}", amt=16)
                wd_toks.append(add("gp", lambda e, f=ftg, q=f4: e.dma_start(
                    wdg[:, q, :], wdT_r[:, f, :]), waits=[wdg_rd[0]], inc="d_wdg", amt=16))

                t_gate = t_up = None
                for s2 in range(2):
                    hcols = slice(s2 * 512, (s2 + 1) * 512)

                    def gatef(e, b=fb, s=hcols):
                        for c in range(NC16):
                            i = e.matmul(psG[:, s], wg[b][:, c, :], ybf[:, c, s],
                                         start=(c == 0), stop=(c == NC16 - 1))
                        return i
                    t_gate = add("pe", gatef,
                                 waits=[wg_tok[fb], ybf_half[s2], psG_tok[s2]], inc="pe")
                    wg_rd[fb] = t_gate
                    ybf_rd[0] = t_gate
                    silu_tok[s2] = add("act", lambda e, s=hcols: e.activation(
                        silu_sb[:, s], psG[:, s], AF.Silu),
                        waits=[t_gate, gmul_tok[s2]])
                    psG_tok[s2] = silu_tok[s2]

                    def upf(e, b=fb, s=hcols):
                        for c in range(NC16):
                            i = e.matmul(psU[:, s], wu[b][:, c, :], ybf[:, c, s],
                                         start=(c == 0), stop=(c == NC16 - 1))
                        return i
                    t_up = add("pe", upf, waits=[wu_tok[fb], ybf_half[s2], psU_tok[s2]],
                               inc="pe")
                    wu_rd[fb] = t_up
                    ybf_rd[0] = t_up
                    gmul_tok[s2] = add("dve", lambda e, q=f4, s=hcols: e.tensor_mul(
                        gbuf[:, q, s], silu_sb[:, s], psU[:, s]),
                        waits=[silu_tok[s2], t_up, g_rd[0]])
                    psU_tok[s2] = gmul_tok[s2]

            for tp in range(2):
                tpc = slice(tp * 512, (tp + 1) * 512)
                for ep in range(8):
                    db = (tp * 8 + ep) % 2

                    def dnf(e, b=db, p=tpc, ee=ep):
                        for e2 in range(2):
                            eg = 2 * ee + e2
                            for f4 in range(GSZ):
                                i = e.matmul(psD[b][:, e2 * 512:(e2 + 1) * 512],
                                             wdg[:, f4, eg * 128:(eg + 1) * 128],
                                             gbuf[:, f4, p],
                                             start=(f4 == 0), stop=(f4 == GSZ - 1))
                        return i
                    t_dn = add("pe", dnf, waits=[wd_toks[-1], gmul_tok[tp], psD_tok[db]],
                               inc="pe")
                    g_rd[0] = t_dn
                    wdg_rd[0] = t_dn
                    t_de = add("dve", lambda e, b=db, p=tpc, ee=ep: e.tensor_add(
                        acc[:, 2 * ee:2 * ee + 2, p],
                        acc[:, 2 * ee:2 * ee + 2, p],
                        psD[b][:, :].rearrange("p (j t) -> p j t", j=2)),
                        waits=[t_dn, ybf_ready[0]])
                    psD_tok[db] = t_de

        dbg_r2_tok = None
        if dbg and cc == 0:
            dbg_r2_tok = add("sc", lambda e: e.dma_start(
                dbg_outs["dr2"].ap(), acc[:, :, :].rearrange("p c t -> p (c t)")),
                waits=[psD_tok[0], psD_tok[1]], inc=dbg_sem(), amt=16)
            acc_zrd[0] = _flat([acc_zrd[0], [dbg_r2_tok]])

        ring_tok[0] = psD_tok[0]
        ring_tok[1] = psD_tok[1]
        ring_tok[2] = [silu_tok[0], silu_tok[1]]
        accS_tok[0] = [gmul_tok[0], gmul_tok[1]]
        accA_tok[0] = [gmul_tok[0], gmul_tok[1]]

        dn_last = g_rd[0]
        tail_hooks = []
        if cc + 1 < NCH:
            ppair_tok[0] = _flat([ppair_tok[0], [dn_last]])
            ppair_tok[1] = _flat([ppair_tok[1], [dn_last]])
            qp_rd[0] = _flat([qp_rd[0], [dn_last]])
            qp_rd[1] = _flat([qp_rd[1], [dn_last]])
            qp_rd[2] = _flat([qp_rd[2], [dn_last]])
            rope_rd[0] = _flat([rope_rd[0], [gmul_tok[0], gmul_tok[1], dn_last]])
            qsq_rd[0] = _flat([qsq_rd[0], [dn_last]])
            partA(0, cc + 1)
            partA_sq(0)
            partB(0)
            tail_hooks = [lambda: partA(1, cc + 1),
                          lambda: (partA_sq(1), partB(1)),
                          lambda: partA(2, cc + 1),
                          lambda: (partA_sq(2), partB(2))]
            prologue_state[0], prologue_state[1] = 3, 3

        for pp in range(C // 256):
            ppc = slice(pp * 256, (pp + 1) * 256)
            gpc = slice(cc * C + pp * 256, cc * C + (pp + 1) * 256)
            toks = emit_ln(
                lambda g_, s=ppc: acc[:, 4 * g_:4 * g_ + 4, s],
                lambda g_: z_stage[:, 4 * g_:4 * g_ + 4, :],
                lambda g_, s=ppc: acc[:, 2 * g_:2 * g_ + 2, s],
                lambda g_: z_stage[:, 2 * g_:2 * g_ + 2, :],
                rbf2, rsq2, lnA2, lnB2, lnC2, lnD2, lnU2,
                [psD_tok[0], psD_tok[1], dbg_r2_tok], [zdma_tok[0]],
                lambda g_, t: None, ln_state2)
            acc_zrd[0] = toks[-1]
            zdma_tok[0] = add("sc", lambda e, g=gpc: e.dma_start(
                zT_r[:, :, g], z_stage[:, :, :]), waits=toks, inc="d_z", amt=16)
            n_out += 1
            if tail_hooks:
                tail_hooks.pop(0)()

        aoT_rd[0] = [dn_last, zdma_tok[0]]
        if cc + 1 >= NCH:
            ppair_tok[0] = _flat([ppair_tok[0], [dn_last]])
            ppair_tok[1] = _flat([ppair_tok[1], [dn_last]])

    plan.wait_only("sc", [zdma_tok[0]])

    sem_names = set()
    for _eng, _fn, waits, inc, _amt, _snap in plan.steps:
        if inc is not None:
            sem_names.add(inc)
        for (s, _v, _p) in waits:
            sem_names.add(s)
    sems = {nm: nc.alloc_semaphore(f"s_{nm}") for nm in sorted(sem_names)}
    with nc.Block() as blk:

        @blk.tensor
        def _(e):
            emit(plan, e, ("pe",), sems, serialize)

        @blk.scalar
        def _(e):
            emit(plan, e, ("act", "sc"), sems, serialize)

        @blk.vector
        def _(e):
            emit(plan, e, ("dve",), sems, serialize)

        @blk.sync
        def _(e):
            emit(plan, e, ("sy",), sems, serialize)

        @blk.gpsimd
        def _(e):
            emit(plan, e, ("gp",), sems, serialize)

    return nc


def make_tables(T=4096):
    hd = 128
    a = np.arange(0, hd, 2, dtype=np.float32)
    pos = np.float32(10000.0) ** ((-2.0 * a - 1.0) / np.float32(hd))
    t = np.arange(T, dtype=np.float32)
    angles = t[:, None] * pos[None, :]
    rot = np.concatenate([angles, angles], axis=-1)
    s4 = np.float32(128.0) ** np.float32(-0.25)
    cosT = (np.cos(rot).astype(np.float32).T * s4).astype(np.float32)
    sinM = (np.sin(rot).astype(np.float32).T * s4).astype(np.float32)
    sinM[:64] *= -1.0
    return cosT.astype(ml_dtypes.bfloat16), sinM.astype(ml_dtypes.bfloat16)


def make_in_maps(x, watt, wout, wgate, wup, wdown, T=4096, n_cores=8):
    watt = np.ascontiguousarray(np.asarray(watt, np.float32))
    wout = np.ascontiguousarray(np.asarray(wout, np.float32))
    wgT = np.ascontiguousarray(np.asarray(wgate, np.float32).T)
    wuT = np.ascontiguousarray(np.asarray(wup, np.float32).T)
    wdT = np.ascontiguousarray(np.asarray(wdown, np.float32).T)
    tabC, tabS = make_tables(T)
    ident = np.eye(128, dtype=ml_dtypes.bfloat16)
    ones = np.ones((128, 128), dtype=ml_dtypes.bfloat16)
    return [{
        "xT": np.ascontiguousarray(np.asarray(x[b], np.float32).T),
        "watt": watt, "wout": wout, "wgT": wgT, "wuT": wuT, "wdT": wdT,
        "tabC": tabC, "tabS": tabS, "identD": ident, "onesD": ones,
    } for b in range(n_cores)]


def kernel(L0_attn_x, L0_attn_w_attn, L0_attn_w_out, L0_ffn_w_gate, L0_ffn_w_up,
           L0_ffn_w_down):
    T = 4096
    in_maps = make_in_maps(L0_attn_x, L0_attn_w_attn, L0_attn_w_out,
                           L0_ffn_w_gate, L0_ffn_w_up, L0_ffn_w_down, T=T)
    nc = build(T=T)
    res = run_bass_kernel_spmd(nc, in_maps, core_ids=list(range(8)))
    out = np.empty((8, T, 2048), np.float32)
    for b in range(8):
        out[b] = res.results[b]["zT"].T
    return out



# version 21
# speedup vs baseline: 1.0021x; 1.0021x over previous
import numpy as np
import ml_dtypes

import concourse.bass as bass
import concourse.mybir as mybir
from concourse.bass_utils import run_bass_kernel_spmd

F32 = mybir.dt.float32
BF16 = mybir.dt.bfloat16
AF = mybir.ActivationFunctionType
ALU = mybir.AluOpType
AX = mybir.AxisListType.X

D = 2048
NC16 = 16
H = 16
HD = 128
FF = 8192
FT = FF // 128
GSZ = 4
EPS = 1e-5

COMPUTE_SEMS = ("pe", "act", "dve")
BODY_OF = {"pe": "tensor", "act": "scalar", "sc": "scalar", "dve": "vector",
           "sy": "sync", "gp": "gpsimd"}


def _flat(waits):
    out = []
    for w in waits:
        if w is None:
            continue
        if isinstance(w, list):
            out.extend(x for x in w if x is not None)
        else:
            out.append(w)
    return out


class Plan:
    def __init__(self):
        self.steps = []
        self.counts = {}

    def add(self, engine, fn, waits=(), inc=None, amt=1):
        if inc is None and engine in ("act", "dve"):
            inc = engine
        tok = None
        snap = dict(self.counts)
        if inc is not None:
            c = self.counts.get(inc, 0) + amt
            self.counts[inc] = c
            tok = (inc, c, engine)
        self.steps.append((engine, fn, _flat(waits), inc, amt, snap))
        return tok

    def wait_only(self, engine, waits):
        self.steps.append((engine, None, _flat(waits), None, 0, dict(self.counts)))


def emit(plan, eng, names, sems, serialize=False):
    last_wait = {}
    for engine, fn, waits, inc, amt, snap in plan.steps:
        if engine not in names:
            continue
        if serialize:
            waits = [(s, v, "__ext__") for s, v in snap.items()]
        elif engine in ("act", "dve"):
            waits = waits + [(engine, snap.get(engine, 0), "__self__")]
        for (sem, val, prod) in waits:
            if not serialize and sem == "pe" and engine == "pe":
                continue
            if last_wait.get(sem, -1) >= val:
                continue
            eng.wait_ge(sems[sem], val)
            last_wait[sem] = val
        if fn is None:
            continue
        inst = fn(eng)
        if inc is not None:
            inst.then_inc(sems[inc], amt)


def build(T=4096, C=1024, dbg=False, serialize=False):
    assert C == 1024 and T % C == 0
    NCH = T // C
    NTK = T // 128
    nc = bass.Bass("TRN2", target_bir_lowering=False)

    xT = nc.dram_tensor("xT", [D, T], F32, kind="ExternalInput")
    watt = nc.dram_tensor("watt", [D, 2304], F32, kind="ExternalInput")
    wout = nc.dram_tensor("wout", [D, D], F32, kind="ExternalInput")
    wgT = nc.dram_tensor("wgT", [D, FF], F32, kind="ExternalInput")
    wuT = nc.dram_tensor("wuT", [D, FF], F32, kind="ExternalInput")
    wdT = nc.dram_tensor("wdT", [FF, D], F32, kind="ExternalInput")
    tabC = nc.dram_tensor("tabC", [128, T], BF16, kind="ExternalInput")
    tabS = nc.dram_tensor("tabS", [128, T], BF16, kind="ExternalInput")
    identD = nc.dram_tensor("identD", [128, 128], BF16, kind="ExternalInput")
    onesD = nc.dram_tensor("onesD", [128, 128], BF16, kind="ExternalInput")
    zT = nc.dram_tensor("zT", [D, T], F32, kind="ExternalOutput")
    woutB = nc.dram_tensor("woutB", [NC16, 128, D], BF16, kind="Internal")

    xT_r = xT.ap().rearrange("(c p) t -> p c t", p=128)
    watt_r = watt.ap().rearrange("(c p) o -> p c o", p=128)
    wout_r = wout.ap().rearrange("(c p) o -> p c o", p=128)

    wgT_r = wgT.ap().rearrange("(c p) f -> p c f", p=128)
    wuT_r = wuT.ap().rearrange("(c p) f -> p c f", p=128)
    wdT_r = wdT.ap().rearrange("(t p) e -> p t e", p=128)
    zT_r = zT.ap().rearrange("(c p) t -> p c t", p=128)

    dbg_outs = {}
    if dbg:
        dbg_outs["dKT"] = nc.dram_tensor("dKT", [128, T], BF16, kind="ExternalOutput")
        dbg_outs["dVtok"] = nc.dram_tensor("dVtok", [128, NTK * 128], BF16, kind="ExternalOutput")
        dbg_outs["dkmax"] = nc.dram_tensor("dkmax", [128, 1], F32, kind="ExternalOutput")
        dbg_outs["daoT"] = nc.dram_tensor("daoT", [128, NC16 * C], BF16, kind="ExternalOutput")
        dbg_outs["dy"] = nc.dram_tensor("dy", [128, NC16 * C], F32, kind="ExternalOutput")
        dbg_outs["dr2"] = nc.dram_tensor("dr2", [128, NC16 * C], F32, kind="ExternalOutput")

    def sb(name, shape, dtype):
        return nc.alloc_sbuf_tensor(name, shape, dtype)

    KT = sb("KT", [128, T], BF16)
    Vtok = sb("Vtok", [128, NTK, 128], BF16)
    ones_sb = sb("ones_sb", [128, 128], BF16)
    kmax2 = sb("kmax2", [128, 1], F32)
    knt = sb("knt", [128, 1], F32)
    qm = sb("qm", [128, 1], F32)
    negM = [sb(f"negM{i}", [128, 1], F32) for i in range(3)]
    ybf = sb("ybf", [128, NC16, C], BF16)
    acc = sb("acc", [128, NC16, C], F32)
    xbf = sb("xbf", [128, NC16, C], BF16)
    wqh = [sb(f"wqh{i}", [128, NC16, 128], BF16) for i in range(2)]
    tabQc = sb("tabQc", [128, C], BF16)
    tabQs = sb("tabQs", [128, C], BF16)

    arena = nc.sbuf_base

    wkv = sb("wkv", [128, NC16, 256], BF16)
    ident = sb("ident", [128, 128], BF16)
    tkc = [sb("tkc0", [128, 512], BF16)]
    tks = [sb("tks0", [128, 512], BF16)]
    rotb1 = sb("rotb1", [128, 512], F32)
    t1b1 = sb("t1b1", [128, 512], F32)
    t2b1 = sb("t2b1", [128, 512], F32)
    vt1 = sb("vt1", [128, 512], BF16)
    ksq1 = sb("ksq1", [128, 512], BF16)
    xbf_p = sb("xbf_p", [128, NC16, 512], BF16)
    p1_end = nc.sbuf_base

    nc.sbuf_base = arena
    aoT = sb("aoT", [128, NC16, C], BF16)
    heads_base = nc.sbuf_base
    P_ring = sb("P_ring", [128, 4, 512], BF16)
    Qp = [sb(f"Qp{i}", [128, C], BF16) for i in range(3)]
    rotb = sb("rotb", [128, 512], BF16)
    t1b = sb("t1b", [128, 512], F32)
    t2b = sb("t2b", [128, 512], BF16)
    qsq = sb("qsq", [128, C], BF16)
    rsum = sb("rsum", [128, 512], F32)
    heads_end = nc.sbuf_base
    nc.sbuf_base = heads_base
    xf = [sb(f"xf{i}", [128, 512], F32) for i in range(2)]
    lnA = sb("lnA", [128, 256], F32)
    lnB = sb("lnB", [128, 256], F32)
    lnC = sb("lnC", [128, 256], F32)
    lnD = sb("lnD", [128, 256], F32)
    rbf = [sb(f"rbf{i}", [128, 4, 256], BF16) for i in range(2)]
    rsq = [sb(f"rsq{i}", [128, 4, 256], BF16) for i in range(2)]
    lnU = sb("lnU", [128, 2, 256], F32)
    ln1_end = nc.sbuf_base
    attn_end = max(heads_end, ln1_end)

    nc.sbuf_base = arena
    wg = [sb(f"wg{i}", [128, NC16, 128], BF16) for i in range(2)]
    wu = [sb(f"wu{i}", [128, NC16, 128], BF16) for i in range(2)]
    wdg = sb("wdg", [128, GSZ, D], BF16)
    gbuf = sb("gbuf", [128, GSZ, C], BF16)
    silu_sb = sb("silu_sb", [128, C], F32)
    ffn_end = nc.sbuf_base
    nc.sbuf_base = arena
    z_stage = sb("z_stage", [128, NC16, 256], F32)
    lnA2 = sb("lnA2", [128, 256], F32)
    lnB2 = sb("lnB2", [128, 256], F32)
    lnC2 = sb("lnC2", [128, 256], F32)
    lnD2 = sb("lnD2", [128, 256], F32)
    rbf2 = [sb(f"rbf2{i}", [128, 4, 256], BF16) for i in range(2)]
    rsq2 = [sb(f"rsq2{i}", [128, 4, 256], BF16) for i in range(2)]
    lnU2 = sb("lnU2", [128, 2, 256], F32)
    ln2_end = nc.sbuf_base

    hi = max(p1_end, attn_end, ffn_end, ln2_end)
    assert hi <= nc.sbuf_top, f"SBUF over budget: {hi} > {nc.sbuf_top}"
    nc.sbuf_base = hi

    pb0 = nc.psum_base
    _ = nc.alloc_psum_tensor("ps_gap", [128, 1024], F32)
    psK = [nc.alloc_psum_tensor(f"psK{i}", [128, 512], F32) for i in range(2)]
    psV = [nc.alloc_psum_tensor(f"psV{i}", [128, 512], F32) for i in range(2)]
    psTR = nc.alloc_psum_tensor("psTR", [128, 512], F32)
    psKN = nc.alloc_psum_tensor("psKN", [128, 512], F32)
    nc.psum_base = pb0
    ring = [nc.alloc_psum_tensor(f"ring{i}", [128, 1024], F32) for i in range(3)]
    accS = nc.alloc_psum_tensor("accS", [128, 512], F32)
    accA = nc.alloc_psum_tensor("accA", [128, 512], F32)
    nc.psum_base = pb0
    psD = [nc.alloc_psum_tensor(f"psD{i}", [128, 1024], F32) for i in range(2)]
    psG = nc.alloc_psum_tensor("psG", [128, 1024], F32)
    psU = nc.alloc_psum_tensor("psU", [128, 1024], F32)
    assert nc.psum_base - pb0 <= 8

    plan = Plan()
    add = plan.add
    ndbg = [0]

    def dbg_sem():
        ndbg[0] += 1
        return f"d_dbg{ndbg[0]}"

    ring_tok = [None, None, None]
    ring_i = [0]

    def ring_acquire():
        idx = ring_i[0] % 3
        ring_i[0] += 1
        return idx, ring_tok[idx]

    ppair_tok = [None, None]
    accS_tok = [None]
    accA_tok = [None]

    t_ones = add("sy", lambda e: e.dma_start(ones_sb[:, :], onesD.ap()), inc="d_ones", amt=16)
    t_ident = add("sy", lambda e: e.dma_start(ident[:, :], identD.ap()), inc="d_ident", amt=16)
    t_wkv = add("gp", lambda e: e.dma_start(wkv[:, :, :], watt_r[:, :, 2048:2304]),
                inc="d_wkv", amt=16)
    t_xbf = add("gp", lambda e: e.dma_start(xbf[:, :, :], xT_r[:, :, 0:C]),
                inc="d_xbf", amt=16)
    t_tabQ1 = add("sy", lambda e: e.dma_start(tabQc[:, :], tabC.ap()[:, 0:C]),
                  inc="d_tabqc", amt=16)
    t_tabQ2 = add("sy", lambda e: e.dma_start(tabQs[:, :], tabS.ap()[:, 0:C]),
                  inc="d_tabqs", amt=16)
    t_wqh = [add("gp", lambda e, i=i: e.dma_start(wqh[i][:, :, :],
                                                  watt_r[:, :, i * 128:(i + 1) * 128]),
                 inc=f"d_wqh{i}", amt=16) for i in range(2)]

    wqh_tok = [None, None]
    negM_rd = [None, None, None]
    qp_rd = [None, None, None]
    qsq_rd = [None]
    xbf_rd = [None]
    ybf_rd = [None]
    aoT_rd = [None]
    tabQ_rd = [None]
    g_rd = [None]
    wdg_rd = [None]
    acc_zrd = [None]
    zdma_tok = [None]
    psD_tok = [None, None]
    psG_tok = [None, None]
    psU_tok = [None, None]
    rope_rd = [None]
    ln_state1 = {"prev_rstd": None, "prev_apply": None}
    ln_state2 = {"prev_rstd": None, "prev_apply": None}
    qp_toks = {}
    negM_tok = {}
    last_ao = [None]
    last_evac = [None]
    ybf_ready = [None]
    silu_tok = [None, None]
    gmul_tok = [None, None]

    def partA(h, cc):
        wb = h % 2
        qb = h % 3
        ridx, rtok = ring_acquire()

        def qfill(e, r=ridx, b=wb):
            for s2 in range(2):
                for c in range(NC16):
                    i = e.matmul(ring[r][:, s2 * 512:(s2 + 1) * 512],
                                 wqh[b][:, c, :], xbf[:, c, s2 * 512:(s2 + 1) * 512],
                                 start=(c == 0), stop=(c == NC16 - 1))
            return i
        t_qf = add("pe", qfill, waits=[t_wqh[wb], t_xbf, rtok], inc="pe")
        wqh_tok[wb] = t_qf
        xbf_rd[0] = t_qf
        if h + 2 < H:
            t_wqh[(h + 2) % 2] = add(
                "gp", lambda e, hh=h + 2: e.dma_start(
                    wqh[hh % 2][:, :, :], watt_r[:, :, hh * 128:(hh + 1) * 128]),
                waits=[wqh_tok[(h + 2) % 2]], inc=f"d_wqh{(h + 2) % 2}", amt=16)

        toks = []
        last_t1 = None
        for s2 in range(2):
            sc_ = slice(s2 * 512, (s2 + 1) * 512)
            t_cp1 = add("dve", lambda e, r=ridx, s=sc_: e.tensor_copy(rotb[0:64, :],
                                                                      ring[r][64:128, s]),
                        waits=[t_qf, rope_rd[0]])
            t_cp2 = add("dve", lambda e, r=ridx, s=sc_: e.tensor_copy(rotb[64:128, :],
                                                                      ring[r][0:64, s]))
            t_t1 = add("dve", lambda e, r=ridx, s=sc_: e.tensor_mul(
                t1b[:, :], ring[r][:, s], tabQc[:, s]),
                waits=[t_qf, t_tabQ1, rope_rd[0]])
            t_t2 = add("dve", lambda e, s=sc_: e.tensor_mul(t2b[:, :], rotb[:, :],
                                                            tabQs[:, s]),
                       waits=[t_tabQ2])
            t_qp = add("dve", lambda e, b=qb, s=sc_: e.tensor_add(Qp[b][:, s], t1b[:, :],
                                                                  t2b[:, :]),
                       waits=[qp_rd[qb]])
            toks.append(t_qp)
            rope_rd[0] = t_qp
            last_t1 = t_t1
        ring_tok[ridx] = last_t1
        tabQ_rd[0] = toks[-1]
        qp_toks[h] = toks

    def partA_sq(h):
        hb = h % 3
        t_sq = add("act", lambda e, b=hb: e.activation(qsq[:, :], Qp[b][:, :], AF.Square),
                   waits=qp_toks[h][:2] + [qsq_rd[0]])
        qp_toks[h].append(t_sq)

    def partB(h):
        hb = h % 3
        r2idx, r2tok = ring_acquire()
        t_sq = qp_toks[h][2]

        def qnfill(e, r=r2idx):
            e.matmul(ring[r][:, 0:512], ones_sb[:, :], qsq[:, 0:512], start=True, stop=True)
            return e.matmul(ring[r][:, 512:1024], ones_sb[:, :], qsq[:, 512:1024],
                            start=True, stop=True)
        t_qn = add("pe", qnfill, waits=[t_sq, t_ones, r2tok], inc="pe")
        qsq_rd[0] = t_qn
        t_qm = add("dve", lambda e, r=r2idx: e.reduce_max(qm[:, :], ring[r][:, :], axis=AX),
                   waits=[t_qn])
        ring_tok[r2idx] = t_qm
        negM_tok[h] = add("dve", lambda e, b=hb: e.tensor_scalar(
            negM[b][:, :], qm[:, :], kmax2[:, :], -0.5, ALU.add, ALU.mult),
            waits=[negM_rd[hb], kmax_tok])

    def attention(h, s2):
        hb = h % 3
        if True:
            scq = slice(s2 * 512, (s2 + 1) * 512)
            t_sum = t_av = None
            exp_q = []

            def emit_sumav(jp, t_exp):
                nonlocal t_sum, t_av

                def savf(e, q=2 * (jp % 2), a=2 * jp, b=2 * jp + 1,
                         first=(jp == 0), last=(jp == NTK // 2 - 1)):
                    e.matmul(accA[:, :], Vtok[:, a, :], P_ring[:, q, :],
                             start=first, stop=False)
                    e.matmul(accA[:, :], Vtok[:, b, :], P_ring[:, q + 1, :],
                             start=False, stop=last)
                    e.matmul(accS[:, :], ones_sb[:, :], P_ring[:, q, :],
                             start=first, stop=False)
                    return e.matmul(accS[:, :], ones_sb[:, :], P_ring[:, q + 1, :],
                                    start=False, stop=last)
                t_av = add("pe", savf,
                           waits=[t_exp, vtok_done,
                                  accS_tok[0] if jp == 0 else None,
                                  accA_tok[0] if jp == 0 else None], inc="pe")
                t_sum = t_av
                ppair_tok[jp % 2] = t_av

            for jp in range(NTK // 2):
                sidx, stok = ring_acquire()
                j0, j1 = 2 * jp, 2 * jp + 1

                def sfill(e, r=sidx, a=j0, b=j1, bb=hb, s=scq):
                    e.matmul(ring[r][:, 0:512], KT[:, a * 128:(a + 1) * 128], Qp[bb][:, s],
                             start=True, stop=True)
                    return e.matmul(ring[r][:, 512:1024], KT[:, b * 128:(b + 1) * 128],
                                    Qp[bb][:, s], start=True, stop=True)
                t_S = add("pe", sfill, waits=[stok, qp_toks[h][s2], kt_done], inc="pe")
                qp_rd[hb] = t_S

                ps = 2 * (jp % 2)
                t_exp = add("act", lambda e, r=sidx, q=ps, b=hb: e.activation(
                    P_ring[:, q:q + 2, :],
                    ring[r][:, :].rearrange("p (j t) -> p j t", j=2),
                    AF.Exp, bias=negM[b][:, :]),
                    waits=[t_S, negM_tok[h], ppair_tok[jp % 2]])
                ring_tok[sidx] = t_exp
                negM_rd[hb] = t_exp
                exp_q.append(t_exp)
                if jp >= 1:
                    emit_sumav(jp - 1, exp_q[jp - 1])
            emit_sumav(NTK // 2 - 1, exp_q[-1])

            t_rs = add("dve", lambda e: e.reciprocal(rsum[:, :], accS[:, :]),
                       waits=[t_sum])
            accS_tok[0] = t_rs
            t_ao = add("dve", lambda e, hh=h, s=scq: e.tensor_mul(
                aoT[:, hh, s], accA[:, :], rsum[:, :]),
                waits=[t_av, aoT_rd[0]])
            accA_tok[0] = t_ao
            last_ao[0] = t_ao

    def bc2(t):
        return bass.AP(t, 0, [[256, 128], [0, 2], [1, 256]])


    NP1 = T // 512
    xp_tok = None
    xp_rd = [None]
    tk_tok = [None, None]
    tk_rd = [None, None]
    psK_rd = [None, None]
    psV_rd = [None, None]
    psTR_rd = None
    psKN_rd = None
    rope1_rd = None
    vt_rd = None
    ksq_rd = None
    kt_done = None
    vtok_done = None
    kmax_tok = None

    prologue0_hook = [lambda: (partA(0, 0), partA_sq(0))]
    for p in range(NP1):
        if p == min(4, NP1 - 1) and prologue0_hook:
            prologue0_hook.pop()()
        pbuf = p % 2
        cols = slice(p * 512, (p + 1) * 512)
        xp_tok = add("gp", lambda e, c=cols: e.dma_start(
            xbf_p[:, :, :], xT_r[:, :, c]),
            waits=[xp_rd[0]], inc="d_xbfp", amt=16)
        tk_tok[0] = add("sy", lambda e, c=cols: e.dma_start(
            tkc[0][:, :], tabC.ap()[:, c]), waits=[tk_rd[0]], inc="d_tkc0", amt=16)
        tk_tok2 = add("sy", lambda e, c=cols: e.dma_start(
            tks[0][:, :], tabS.ap()[:, c]), waits=[tk_rd[0]], inc="d_tks0", amt=16)

        def kfill(e, b=pbuf):
            for c in range(NC16):
                i = e.matmul(psK[b][:, :], wkv[:, c, 0:128], xbf_p[:, c, :],
                             start=(c == 0), stop=(c == NC16 - 1))
            return i
        t_kf = add("pe", kfill, waits=[xp_tok, t_wkv, psK_rd[pbuf]], inc="pe")

        def vfill(e, b=pbuf):
            for c in range(NC16):
                i = e.matmul(psV[b][:, :], wkv[:, c, 128:256], xbf_p[:, c, :],
                             start=(c == 0), stop=(c == NC16 - 1))
            return i
        t_vf = add("pe", vfill, waits=[xp_tok, t_wkv, psV_rd[pbuf]], inc="pe")
        xp_rd[0] = t_vf

        t_cp1 = add("act", lambda e, b=pbuf: e.copy(rotb1[0:64, :], psK[b][64:128, :]),
                    waits=[t_kf, rope1_rd])
        t_cp2 = add("act", lambda e, b=pbuf: e.copy(rotb1[64:128, :], psK[b][0:64, :]))
        t_t1 = add("dve", lambda e, b=pbuf: e.tensor_mul(t1b1[:, :], psK[b][:, :],
                                                         tkc[0][:, :]),
                   waits=[t_kf, t_cp2, tk_tok[0], rope1_rd])
        psK_rd[pbuf] = [t_t1, t_cp2]
        t_t2 = add("dve", lambda e: e.tensor_mul(t2b1[:, :], rotb1[:, :],
                                                 tks[0][:, :]),
                   waits=[tk_tok2])
        tk_rd[0] = t_t2
        t_kt = add("dve", lambda e, c=cols: e.tensor_add(KT[:, c], t1b1[:, :], t2b1[:, :]))
        rope1_rd = t_kt
        kt_done = t_kt

        t_vt = add("act", lambda e, b=pbuf: e.copy(vt1[:, :], psV[b][:, :]),
                   waits=[t_vf, vt_rd])
        psV_rd[pbuf] = t_vt

        def trfill(e):
            for q in range(4):
                i = e.matmul(psTR[:, q * 128:(q + 1) * 128], vt1[:, q * 128:(q + 1) * 128],
                             ident[:, :], start=True, stop=True)
            return i
        t_tr = add("pe", trfill, waits=[t_vt, t_ident, psTR_rd], inc="pe")
        vt_rd = t_tr
        t_ev = add("act", lambda e, q=p: e.copy(
            Vtok[:, q * 4:(q + 1) * 4, :],
            psTR[:, :].rearrange("p (j h) -> p j h", j=4)), waits=[t_tr])
        psTR_rd = t_ev
        vtok_done = t_ev

        t_ks = add("act", lambda e, c=cols: e.activation(ksq1[:, :], KT[:, c], AF.Square),
                   waits=[t_kt, ksq_rd])
        t_kn = add("pe", lambda e: e.matmul(psKN[:, :], ones_sb[:, :], ksq1[:, :],
                                            start=True, stop=True),
                   waits=[t_ks, t_ones, psKN_rd], inc="pe")
        ksq_rd = t_kn
        t_red = add("dve", lambda e: e.reduce_max(knt[:, :], psKN[:, :], axis=AX),
                    waits=[t_kn])
        psKN_rd = t_red
        if p == 0:
            kmax_tok = add("dve", lambda e: e.tensor_copy(kmax2[:, :], knt[:, :]))
        else:
            kmax_tok = add("dve", lambda e: e.tensor_max(kmax2[:, :], kmax2[:, :], knt[:, :]))

    wo_prep = None
    for q in range(NC16):
        wo_prep = add("gp", lambda e, q=q: e.dma_start(
            woutB.ap()[q].rearrange("p (c o) -> p c o", c=NC16),
            wout_r[:, :, q * 128:(q + 1) * 128]), inc="d_woprep", amt=16)

    ring_tok[1] = _flat([ring_tok[1], psK_rd[0], psK_rd[1]])
    ring_tok[2] = _flat([ring_tok[2], [psV_rd[0], psV_rd[1]]])
    accS_tok[0] = _flat([accS_tok[0], [psTR_rd]])
    accA_tok[0] = _flat([accA_tok[0], [psKN_rd]])

    if dbg:
        add("sc", lambda e: e.dma_start(dbg_outs["dKT"].ap(), KT[:, :]),
            waits=[kt_done], inc=dbg_sem(), amt=16)
        add("sc", lambda e: e.dma_start(
            dbg_outs["dVtok"].ap(), Vtok[:, :, :].rearrange("p j h -> p (j h)")),
            waits=[vtok_done], inc=dbg_sem(), amt=16)
        add("sc", lambda e: e.dma_start(dbg_outs["dkmax"].ap(), kmax2[:, :]),
            waits=[kmax_tok], inc=dbg_sem(), amt=16)

    def ln_stats(src4, rbf_, rsq_, stats_waits):
        rb_rd = [None, None]
        t_mm = None
        for g in range(4):
            gb = g % 2
            t_rb = add("act", lambda e, g_=g, b=gb: e.copy(rbf_[b][:, :, :], src4(g_)),
                       waits=stats_waits + [rb_rd[gb]])
            t_rq = add("act", lambda e, b=gb: e.activation(rsq_[b][:, :, :],
                                                           rbf_[b][:, :, :], AF.Square))

            def statmm(e, b=gb, first=(g == 0), last=(g == 3)):
                for j in range(4):
                    e.matmul(accS[:, 0:256], ones_sb[:, :], rbf_[b][:, j, :],
                             start=(first and j == 0), stop=(last and j == 3))
                    i = e.matmul(accA[:, 0:256], ones_sb[:, :], rsq_[b][:, j, :],
                                 start=(first and j == 0), stop=(last and j == 3))
                return i
            t_mm = add("pe", statmm,
                       waits=[t_rq, accS_tok[0] if g == 0 else None,
                              accA_tok[0] if g == 0 else None], inc="pe")
            rb_rd[gb] = t_mm
        return t_mm

    def ln_chain(t_mm, A, B, Cc, Dd, state):
        t_A = add("dve", lambda e: e.tensor_scalar_mul(A[:, :], accS[:, 0:256], 1.0 / D),
                  waits=[t_mm, state["prev_rstd"]])
        t_B = add("dve", lambda e: e.tensor_scalar_mul(B[:, :], accA[:, 0:256], 1.0 / D))
        accS_tok[0] = t_A
        accA_tok[0] = t_B
        add("dve", lambda e: e.tensor_mul(Cc[:, :], A[:, :], A[:, :]))
        t_var = add("dve", lambda e: e.scalar_tensor_tensor(
            Dd[:, :], B[:, :], float(EPS), Cc[:, :], ALU.add, ALU.subtract))
        t_ln = add("act", lambda e: e.activation(B[:, :], Dd[:, :], AF.Ln),
                   waits=[t_var])
        t_rstd = add("act", lambda e: e.activation(Cc[:, :], B[:, :], AF.Exp, scale=-0.5))
        state["prev_rstd"] = t_rstd
        add("dve", lambda e: e.tensor_mul(Dd[:, :], A[:, :], Cc[:, :]), waits=[t_rstd])
        return t_rstd

    def ln_apply(src2, apply_out2, Cc, Dd, U, t_rstd, apply_waits, extra4):
        out_toks = []
        for g in range(8):
            add("dve", lambda e, g_=g: e.tensor_mul(U[:, :, :], src2(g_), bc2(Cc)),
                waits=[t_rstd] + apply_waits)
            t_y = add("dve", lambda e, g_=g: e.tensor_sub(apply_out2(g_), U[:, :, :],
                                                          bc2(Dd)))
            if g % 2 == 1:
                ee = extra4(g // 2, t_y)
                out_toks.append(ee if ee is not None else t_y)
        return out_toks

    def emit_ln(src4, apply_out4, src2, apply_out2, rbf_, rsq_, A, B, Cc, Dd, U,
                stats_waits, apply_waits, extra4, state):
        t_mm = ln_stats(src4, rbf_, rsq_, stats_waits)
        t_rstd = ln_chain(t_mm, A, B, Cc, Dd, state)
        return ln_apply(src2, apply_out2, Cc, Dd, U, t_rstd, apply_waits, extra4)

    n_out = 0
    prologue_state = [1, 0]
    for cc in range(NCH):
        pa, pb = prologue_state
        for h in range(H):
            if h == 0 and h >= pb:
                partB(h)
            if h + 1 < H and h + 1 >= pa:
                partA(h + 1, cc)
            attention(h, 0)
            if h + 1 < H and h + 1 >= pa:
                partA_sq(h + 1)
            if h + 1 < H and h + 1 >= pb:
                partB(h + 1)
            attention(h, 1)

        wo_tok = [None, None]
        xf_tok = [None, None]
        xf_rd = [last_ao[0], last_ao[0]]
        last_pj = [None]
        ybf_half = [None, None]
        evac_of_piece = [None, None]

        def proj_piece(piece):
            pc = slice(piece * 512, (piece + 1) * 512)
            gpc0 = cc * C + piece * 512
            for e_t in range(NC16):
                eb = e_t % 2
                wo_tok[eb] = add("sy", lambda e, et=e_t: e.dma_start(
                    wqh[et % 2][:, :, :],
                    woutB.ap()[et].rearrange("p (c o) -> p c o", c=NC16)),
                    waits=[wqh_tok[eb], wo_prep], inc=f"d_wo{eb}", amt=16)
                xb = e_t % 2
                xf_tok[xb] = add("sc", lambda e, et=e_t, g=slice(gpc0, gpc0 + 512),
                                 b=xb: e.dma_start(
                    xf[b][:, :], xT_r[:, et, g]), waits=[xf_rd[xb]], inc=f"d_xf{xb}", amt=16)
                ridx, rtok = ring_acquire()

                def pjfill(e, r=ridx, b=eb, p=pc):
                    for c in range(NC16):
                        i = e.matmul(ring[r][:, 0:512], wqh[b][:, c, :], aoT[:, c, p],
                                     start=(c == 0), stop=(c == NC16 - 1))
                    return i
                t_pj = add("pe", pjfill, waits=[wo_tok[eb], last_ao[0], rtok], inc="pe")
                wqh_tok[eb] = t_pj
                aoT_rd[0] = t_pj
                last_pj[0] = t_pj
                t_ev = add("dve", lambda e, r=ridx, et=e_t, p=pc, b=xb: e.tensor_add(
                    acc[:, et, p], ring[r][:, 0:512], xf[b][:, :]),
                    waits=[t_pj, xf_tok[xb], acc_zrd[0]])
                ring_tok[ridx] = t_ev
                xf_rd[xb] = t_ev
                evac_of_piece[piece] = t_ev

        def ln1_sub(piece, sub):
            ppc = slice(piece * 512 + sub * 256, piece * 512 + sub * 256 + 256)

            def extra1(g_, t_y, s=ppc):
                return add("act", lambda e, g2=g_, s=s: e.copy(
                    ybf[:, 4 * g2:4 * g2 + 4, s], acc[:, 4 * g2:4 * g2 + 4, s]),
                    waits=[t_y, ybf_rd[0]])
            toks = emit_ln(
                lambda g_, s=ppc: acc[:, 4 * g_:4 * g_ + 4, s],
                lambda g_, s=ppc: acc[:, 4 * g_:4 * g_ + 4, s],
                lambda g_, s=ppc: acc[:, 2 * g_:2 * g_ + 2, s],
                lambda g_, s=ppc: acc[:, 2 * g_:2 * g_ + 2, s],
                rbf, rsq, lnA, lnB, lnC, lnD, lnU, [evac_of_piece[piece]], [],
                extra1, ln_state1)
            return toks[-1]

        proj_piece(0)
        ln1_sub(0, 0)
        proj_piece(1)
        ybf_half[0] = ln1_sub(0, 1)
        ln1_sub(1, 0)
        ybf_half[1] = ln1_sub(1, 1)
        ybf_ready[0] = ybf_half[1]

        dbg_ao_tok = None
        if dbg and cc == 0:
            dbg_ao_tok = add("sc", lambda e: e.dma_start(
                dbg_outs["daoT"].ap(), aoT[:, :, :].rearrange("p c t -> p (c t)")),
                waits=[aoT_rd[0], last_ao[0]], inc=dbg_sem(), amt=16)
            dbg_y_tok = add("sc", lambda e: e.dma_start(
                dbg_outs["dy"].ap(), acc[:, :, :].rearrange("p c t -> p (c t)")),
                waits=[ybf_ready[0]], inc=dbg_sem(), amt=16)
            ybf_ready[0] = [ybf_ready[0], dbg_y_tok]

        if cc + 1 < NCH:
            nch = slice((cc + 1) * C, (cc + 2) * C)
            t_xbf = add("gp", lambda e, s=nch: e.dma_start(xbf[:, :, :], xT_r[:, :, s]),
                        waits=[xbf_rd[0]], inc="d_xbf", amt=16)
            t_tabQ1 = add("sy", lambda e, s=nch: e.dma_start(tabQc[:, :], tabC.ap()[:, s]),
                          waits=[tabQ_rd[0]], inc="d_tabqc", amt=16)
            t_tabQ2 = add("sy", lambda e, s=nch: e.dma_start(tabQs[:, :], tabS.ap()[:, s]),
                          waits=[tabQ_rd[0]], inc="d_tabqs", amt=16)
            for i in range(2):
                t_wqh[i] = add("gp", lambda e, i=i: e.dma_start(
                    wqh[i][:, :, :], watt_r[:, :, i * 128:(i + 1) * 128]),
                    waits=[wqh_tok[i]], inc=f"d_wqh{i}", amt=16)

        ffn_war = [last_pj[0], zdma_tok[0], acc_zrd[0], dbg_ao_tok]
        psG_tok[0] = _flat([psG_tok[0], ring_tok[2]])
        psG_tok[1] = _flat([psG_tok[1], ring_tok[2]])
        psU_tok[0] = _flat([psU_tok[0], [accS_tok[0], accA_tok[0]]])
        psU_tok[1] = _flat([psU_tok[1], [accS_tok[0], accA_tok[0]]])
        psD_tok[0] = _flat([psD_tok[0], ring_tok[0]])
        psD_tok[1] = _flat([psD_tok[1], ring_tok[1]])
        wg_tok = [None, None]
        wu_tok = [None, None]
        wg_rd = [ffn_war, ffn_war]
        wu_rd = [ffn_war, ffn_war]
        wdg_rd[0] = _flat([wdg_rd[0], ffn_war])
        for fg in range(FT // GSZ):
            wd_toks = []
            for f4 in range(GSZ):
                ftg = fg * GSZ + f4
                fb = ftg % 2
                wg_tok[fb] = add("gp", lambda e, f=ftg: e.dma_start(
                    wg[f % 2][:, :, :], wgT_r[:, :, f * 128:(f + 1) * 128]),
                    waits=[wg_rd[fb]], inc=f"d_wg{fb}", amt=16)
                wu_tok[fb] = add("gp", lambda e, f=ftg: e.dma_start(
                    wu[f % 2][:, :, :], wuT_r[:, :, f * 128:(f + 1) * 128]),
                    waits=[wu_rd[fb]], inc=f"d_wu{fb}", amt=16)
                wd_toks.append(add("gp", lambda e, f=ftg, q=f4: e.dma_start(
                    wdg[:, q, :], wdT_r[:, f, :]), waits=[wdg_rd[0]], inc="d_wdg", amt=16))

                t_gate = t_up = None
                for s2 in range(2):
                    hcols = slice(s2 * 512, (s2 + 1) * 512)

                    def gatef(e, b=fb, s=hcols):
                        for c in range(NC16):
                            i = e.matmul(psG[:, s], wg[b][:, c, :], ybf[:, c, s],
                                         start=(c == 0), stop=(c == NC16 - 1))
                        return i
                    t_gate = add("pe", gatef,
                                 waits=[wg_tok[fb], ybf_half[s2], psG_tok[s2]], inc="pe")
                    wg_rd[fb] = t_gate
                    ybf_rd[0] = t_gate
                    silu_tok[s2] = add("act", lambda e, s=hcols: e.activation(
                        silu_sb[:, s], psG[:, s], AF.Silu),
                        waits=[t_gate, gmul_tok[s2]])
                    psG_tok[s2] = silu_tok[s2]

                    def upf(e, b=fb, s=hcols):
                        for c in range(NC16):
                            i = e.matmul(psU[:, s], wu[b][:, c, :], ybf[:, c, s],
                                         start=(c == 0), stop=(c == NC16 - 1))
                        return i
                    t_up = add("pe", upf, waits=[wu_tok[fb], ybf_half[s2], psU_tok[s2]],
                               inc="pe")
                    wu_rd[fb] = t_up
                    ybf_rd[0] = t_up
                    gmul_tok[s2] = add("dve", lambda e, q=f4, s=hcols: e.tensor_mul(
                        gbuf[:, q, s], silu_sb[:, s], psU[:, s]),
                        waits=[silu_tok[s2], t_up, g_rd[0]])
                    psU_tok[s2] = gmul_tok[s2]

            for tp in range(2):
                tpc = slice(tp * 512, (tp + 1) * 512)
                for ep in range(8):
                    db = (tp * 8 + ep) % 2

                    def dnf(e, b=db, p=tpc, ee=ep):
                        for e2 in range(2):
                            eg = 2 * ee + e2
                            for f4 in range(GSZ):
                                i = e.matmul(psD[b][:, e2 * 512:(e2 + 1) * 512],
                                             wdg[:, f4, eg * 128:(eg + 1) * 128],
                                             gbuf[:, f4, p],
                                             start=(f4 == 0), stop=(f4 == GSZ - 1))
                        return i
                    t_dn = add("pe", dnf, waits=[wd_toks[-1], gmul_tok[tp], psD_tok[db]],
                               inc="pe")
                    g_rd[0] = t_dn
                    wdg_rd[0] = t_dn
                    t_de = add("dve", lambda e, b=db, p=tpc, ee=ep: e.tensor_add(
                        acc[:, 2 * ee:2 * ee + 2, p],
                        acc[:, 2 * ee:2 * ee + 2, p],
                        psD[b][:, :].rearrange("p (j t) -> p j t", j=2)),
                        waits=[t_dn, ybf_ready[0]])
                    psD_tok[db] = t_de

        dbg_r2_tok = None
        if dbg and cc == 0:
            dbg_r2_tok = add("sc", lambda e: e.dma_start(
                dbg_outs["dr2"].ap(), acc[:, :, :].rearrange("p c t -> p (c t)")),
                waits=[psD_tok[0], psD_tok[1]], inc=dbg_sem(), amt=16)
            acc_zrd[0] = _flat([acc_zrd[0], [dbg_r2_tok]])

        ring_tok[0] = psD_tok[0]
        ring_tok[1] = psD_tok[1]
        ring_tok[2] = [silu_tok[0], silu_tok[1]]
        accS_tok[0] = [gmul_tok[0], gmul_tok[1]]
        accA_tok[0] = [gmul_tok[0], gmul_tok[1]]

        dn_last = g_rd[0]
        tail_hooks = []
        if cc + 1 < NCH:
            ppair_tok[0] = _flat([ppair_tok[0], [dn_last]])
            ppair_tok[1] = _flat([ppair_tok[1], [dn_last]])
            qp_rd[0] = _flat([qp_rd[0], [dn_last]])
            qp_rd[1] = _flat([qp_rd[1], [dn_last]])
            qp_rd[2] = _flat([qp_rd[2], [dn_last]])
            rope_rd[0] = _flat([rope_rd[0], [gmul_tok[0], gmul_tok[1], dn_last]])
            qsq_rd[0] = _flat([qsq_rd[0], [dn_last]])
            partA(0, cc + 1)
            partA_sq(0)
            partB(0)
            tail_hooks = [lambda: partA(1, cc + 1),
                          lambda: (partA_sq(1), partB(1)),
                          lambda: partA(2, cc + 1),
                          lambda: (partA_sq(2), partB(2))]
            prologue_state[0], prologue_state[1] = 3, 3

        NPP = C // 256

        def ln2_src4(pp):
            s = slice(pp * 256, (pp + 1) * 256)
            return lambda g_: acc[:, 4 * g_:4 * g_ + 4, s]

        t_mm_next = ln_stats(ln2_src4(0), rbf2, rsq2,
                             [psD_tok[0], psD_tok[1], dbg_r2_tok])
        for pp in range(NPP):
            ppc = slice(pp * 256, (pp + 1) * 256)
            gpc = slice(cc * C + pp * 256, cc * C + (pp + 1) * 256)
            t_rstd = ln_chain(t_mm_next, lnA2, lnB2, lnC2, lnD2, ln_state2)
            if pp + 1 < NPP:
                t_mm_next = ln_stats(ln2_src4(pp + 1), rbf2, rsq2,
                                     [psD_tok[0], psD_tok[1], dbg_r2_tok])
            toks = ln_apply(
                lambda g_, s=ppc: acc[:, 2 * g_:2 * g_ + 2, s],
                lambda g_: z_stage[:, 2 * g_:2 * g_ + 2, :],
                lnC2, lnD2, lnU2, t_rstd, [zdma_tok[0]], lambda g_, t: None)
            acc_zrd[0] = toks[-1]
            zdma_tok[0] = add("sc", lambda e, g=gpc: e.dma_start(
                zT_r[:, :, g], z_stage[:, :, :]), waits=toks, inc="d_z", amt=16)
            n_out += 1
            if tail_hooks:
                tail_hooks.pop(0)()

        aoT_rd[0] = [dn_last, zdma_tok[0]]
        if cc + 1 >= NCH:
            ppair_tok[0] = _flat([ppair_tok[0], [dn_last]])
            ppair_tok[1] = _flat([ppair_tok[1], [dn_last]])

    plan.wait_only("sc", [zdma_tok[0]])

    sem_names = set()
    for _eng, _fn, waits, inc, _amt, _snap in plan.steps:
        if inc is not None:
            sem_names.add(inc)
        for (s, _v, _p) in waits:
            sem_names.add(s)
    sems = {nm: nc.alloc_semaphore(f"s_{nm}") for nm in sorted(sem_names)}
    with nc.Block() as blk:

        @blk.tensor
        def _(e):
            emit(plan, e, ("pe",), sems, serialize)

        @blk.scalar
        def _(e):
            emit(plan, e, ("act", "sc"), sems, serialize)

        @blk.vector
        def _(e):
            emit(plan, e, ("dve",), sems, serialize)

        @blk.sync
        def _(e):
            emit(plan, e, ("sy",), sems, serialize)

        @blk.gpsimd
        def _(e):
            emit(plan, e, ("gp",), sems, serialize)

    return nc


def make_tables(T=4096):
    hd = 128
    a = np.arange(0, hd, 2, dtype=np.float32)
    pos = np.float32(10000.0) ** ((-2.0 * a - 1.0) / np.float32(hd))
    t = np.arange(T, dtype=np.float32)
    angles = t[:, None] * pos[None, :]
    rot = np.concatenate([angles, angles], axis=-1)
    s4 = np.float32(128.0) ** np.float32(-0.25)
    cosT = (np.cos(rot).astype(np.float32).T * s4).astype(np.float32)
    sinM = (np.sin(rot).astype(np.float32).T * s4).astype(np.float32)
    sinM[:64] *= -1.0
    return cosT.astype(ml_dtypes.bfloat16), sinM.astype(ml_dtypes.bfloat16)


def make_in_maps(x, watt, wout, wgate, wup, wdown, T=4096, n_cores=8):
    watt = np.ascontiguousarray(np.asarray(watt, np.float32))
    wout = np.ascontiguousarray(np.asarray(wout, np.float32))
    wgT = np.ascontiguousarray(np.asarray(wgate, np.float32).T)
    wuT = np.ascontiguousarray(np.asarray(wup, np.float32).T)
    wdT = np.ascontiguousarray(np.asarray(wdown, np.float32).T)
    tabC, tabS = make_tables(T)
    ident = np.eye(128, dtype=ml_dtypes.bfloat16)
    ones = np.ones((128, 128), dtype=ml_dtypes.bfloat16)
    return [{
        "xT": np.ascontiguousarray(np.asarray(x[b], np.float32).T),
        "watt": watt, "wout": wout, "wgT": wgT, "wuT": wuT, "wdT": wdT,
        "tabC": tabC, "tabS": tabS, "identD": ident, "onesD": ones,
    } for b in range(n_cores)]


def kernel(L0_attn_x, L0_attn_w_attn, L0_attn_w_out, L0_ffn_w_gate, L0_ffn_w_up,
           L0_ffn_w_down):
    T = 4096
    in_maps = make_in_maps(L0_attn_x, L0_attn_w_attn, L0_attn_w_out,
                           L0_ffn_w_gate, L0_ffn_w_up, L0_ffn_w_down, T=T)
    nc = build(T=T)
    res = run_bass_kernel_spmd(nc, in_maps, core_ids=list(range(8)))
    out = np.empty((8, T, 2048), np.float32)
    for b in range(8):
        out[b] = res.results[b]["zT"].T
    return out



# version 22
# speedup vs baseline: 1.0097x; 1.0077x over previous
import numpy as np
import ml_dtypes

import concourse.bass as bass
import concourse.mybir as mybir
from concourse.bass_utils import run_bass_kernel_spmd

F32 = mybir.dt.float32
BF16 = mybir.dt.bfloat16
AF = mybir.ActivationFunctionType
ALU = mybir.AluOpType
AX = mybir.AxisListType.X

D = 2048
NC16 = 16
H = 16
HD = 128
FF = 8192
FT = FF // 128
GSZ = 4
EPS = 1e-5

COMPUTE_SEMS = ("pe", "act", "dve")
BODY_OF = {"pe": "tensor", "act": "scalar", "sc": "scalar", "dve": "vector",
           "sy": "sync", "gp": "gpsimd"}


def _flat(waits):
    out = []
    for w in waits:
        if w is None:
            continue
        if isinstance(w, list):
            out.extend(x for x in w if x is not None)
        else:
            out.append(w)
    return out


class Plan:
    def __init__(self):
        self.steps = []
        self.counts = {}

    def add(self, engine, fn, waits=(), inc=None, amt=1):
        if inc is None and engine in ("act", "dve"):
            inc = engine
        tok = None
        snap = dict(self.counts)
        if inc is not None:
            c = self.counts.get(inc, 0) + amt
            self.counts[inc] = c
            tok = (inc, c, engine)
        self.steps.append((engine, fn, _flat(waits), inc, amt, snap))
        return tok

    def wait_only(self, engine, waits):
        self.steps.append((engine, None, _flat(waits), None, 0, dict(self.counts)))


def emit(plan, eng, names, sems, serialize=False):
    last_wait = {}
    for engine, fn, waits, inc, amt, snap in plan.steps:
        if engine not in names:
            continue
        if serialize:
            waits = [(s, v, "__ext__") for s, v in snap.items()]
        elif engine in ("act", "dve"):
            waits = waits + [(engine, snap.get(engine, 0), "__self__")]
        for (sem, val, prod) in waits:
            if not serialize and sem == "pe" and engine == "pe":
                continue
            if last_wait.get(sem, -1) >= val:
                continue
            eng.wait_ge(sems[sem], val)
            last_wait[sem] = val
        if fn is None:
            continue
        inst = fn(eng)
        if inc is not None:
            inst.then_inc(sems[inc], amt)


def build(T=4096, C=1024, dbg=False, serialize=False):
    assert C == 1024 and T % C == 0
    NCH = T // C
    NTK = T // 128
    nc = bass.Bass("TRN2", target_bir_lowering=False)

    xT = nc.dram_tensor("xT", [D, T], F32, kind="ExternalInput")
    watt = nc.dram_tensor("watt", [D, 2304], F32, kind="ExternalInput")
    wout = nc.dram_tensor("wout", [D, D], F32, kind="ExternalInput")
    wgT = nc.dram_tensor("wgT", [D, FF], F32, kind="ExternalInput")
    wuT = nc.dram_tensor("wuT", [D, FF], F32, kind="ExternalInput")
    wdT = nc.dram_tensor("wdT", [FF, D], F32, kind="ExternalInput")
    tabC = nc.dram_tensor("tabC", [128, T], BF16, kind="ExternalInput")
    tabS = nc.dram_tensor("tabS", [128, T], BF16, kind="ExternalInput")
    identD = nc.dram_tensor("identD", [128, 128], BF16, kind="ExternalInput")
    onesD = nc.dram_tensor("onesD", [128, 128], BF16, kind="ExternalInput")
    zT = nc.dram_tensor("zT", [D, T], F32, kind="ExternalOutput")
    woutB = nc.dram_tensor("woutB", [NC16, 128, D], BF16, kind="Internal")

    xT_r = xT.ap().rearrange("(c p) t -> p c t", p=128)
    watt_r = watt.ap().rearrange("(c p) o -> p c o", p=128)
    wout_r = wout.ap().rearrange("(c p) o -> p c o", p=128)

    wgT_r = wgT.ap().rearrange("(c p) f -> p c f", p=128)
    wuT_r = wuT.ap().rearrange("(c p) f -> p c f", p=128)
    wdT_r = wdT.ap().rearrange("(t p) e -> p t e", p=128)
    zT_r = zT.ap().rearrange("(c p) t -> p c t", p=128)

    dbg_outs = {}
    if dbg:
        dbg_outs["dKT"] = nc.dram_tensor("dKT", [128, T], BF16, kind="ExternalOutput")
        dbg_outs["dVtok"] = nc.dram_tensor("dVtok", [128, NTK * 128], BF16, kind="ExternalOutput")
        dbg_outs["dkmax"] = nc.dram_tensor("dkmax", [128, 1], F32, kind="ExternalOutput")
        dbg_outs["daoT"] = nc.dram_tensor("daoT", [128, NC16 * C], BF16, kind="ExternalOutput")
        dbg_outs["dy"] = nc.dram_tensor("dy", [128, NC16 * C], F32, kind="ExternalOutput")
        dbg_outs["dr2"] = nc.dram_tensor("dr2", [128, NC16 * C], F32, kind="ExternalOutput")

    def sb(name, shape, dtype):
        return nc.alloc_sbuf_tensor(name, shape, dtype)

    KT = sb("KT", [128, T], BF16)
    Vtok = sb("Vtok", [128, NTK, 128], BF16)
    ones_sb = sb("ones_sb", [128, 128], BF16)
    kmax2 = sb("kmax2", [128, 1], F32)
    knt = sb("knt", [128, 1], F32)
    qm = sb("qm", [128, 1], F32)
    negM = [sb(f"negM{i}", [128, 1], F32) for i in range(3)]
    ybf = sb("ybf", [128, NC16, C], BF16)
    acc = sb("acc", [128, NC16, C], F32)
    xbf = sb("xbf", [128, NC16, C], BF16)
    wqh = [sb(f"wqh{i}", [128, NC16, 128], BF16) for i in range(2)]
    tabQc = sb("tabQc", [128, C], BF16)
    tabQs = sb("tabQs", [128, C], BF16)

    arena = nc.sbuf_base

    wkv = sb("wkv", [128, NC16, 256], BF16)
    ident = sb("ident", [128, 128], BF16)
    tkc = [sb("tkc0", [128, 512], BF16)]
    tks = [sb("tks0", [128, 512], BF16)]
    rotb1 = sb("rotb1", [128, 512], F32)
    t1b1 = sb("t1b1", [128, 512], F32)
    t2b1 = sb("t2b1", [128, 512], F32)
    vt1 = sb("vt1", [128, 512], BF16)
    ksq1 = sb("ksq1", [128, 512], BF16)
    xbf_p = sb("xbf_p", [128, NC16, 512], BF16)
    p1_end = nc.sbuf_base

    nc.sbuf_base = arena
    aoT = sb("aoT", [128, NC16, C], BF16)
    heads_base = nc.sbuf_base
    P_ring = sb("P_ring", [128, 4, 512], BF16)
    Qp = [sb(f"Qp{i}", [128, C], BF16) for i in range(3)]
    rotb = sb("rotb", [128, 512], BF16)
    t1b = sb("t1b", [128, 512], F32)
    t2b = sb("t2b", [128, 512], BF16)
    qsq = sb("qsq", [128, C], BF16)
    rsum = sb("rsum", [128, 512], F32)
    heads_end = nc.sbuf_base
    nc.sbuf_base = heads_base
    xf = [sb(f"xf{i}", [128, 512], F32) for i in range(2)]
    lnA = sb("lnA", [128, 256], F32)
    lnB = sb("lnB", [128, 256], F32)
    lnC = sb("lnC", [128, 256], F32)
    lnD = sb("lnD", [128, 256], F32)
    rbf = [sb(f"rbf{i}", [128, 4, 256], BF16) for i in range(2)]
    rsq = [sb(f"rsq{i}", [128, 4, 256], BF16) for i in range(2)]
    lnU = sb("lnU", [128, 2, 256], F32)
    ln1_end = nc.sbuf_base
    attn_end = max(heads_end, ln1_end)

    nc.sbuf_base = arena
    wg = [sb(f"wg{i}", [128, NC16, 128], BF16) for i in range(2)]
    wu = [sb(f"wu{i}", [128, NC16, 128], BF16) for i in range(2)]
    wdg = sb("wdg", [128, GSZ, D], BF16)
    gbuf = sb("gbuf", [128, GSZ, C], BF16)
    silu_sb = sb("silu_sb", [128, C], F32)
    ffn_end = nc.sbuf_base
    nc.sbuf_base = arena
    z_stage = sb("z_stage", [128, NC16, 256], F32)
    lnA2 = sb("lnA2", [128, 256], F32)
    lnB2 = sb("lnB2", [128, 256], F32)
    lnC2 = sb("lnC2", [128, 256], F32)
    lnD2 = sb("lnD2", [128, 256], F32)
    rbf2 = [sb(f"rbf2{i}", [128, 4, 256], BF16) for i in range(2)]
    rsq2 = [sb(f"rsq2{i}", [128, 4, 256], BF16) for i in range(2)]
    lnU2 = sb("lnU2", [128, 2, 256], F32)
    ln2_end = nc.sbuf_base

    hi = max(p1_end, attn_end, ffn_end, ln2_end)
    assert hi <= nc.sbuf_top, f"SBUF over budget: {hi} > {nc.sbuf_top}"
    nc.sbuf_base = hi

    pb0 = nc.psum_base
    _ = nc.alloc_psum_tensor("ps_gap", [128, 1024], F32)
    psK = [nc.alloc_psum_tensor(f"psK{i}", [128, 512], F32) for i in range(2)]
    psV = [nc.alloc_psum_tensor(f"psV{i}", [128, 512], F32) for i in range(2)]
    psTR = nc.alloc_psum_tensor("psTR", [128, 512], F32)
    psKN = nc.alloc_psum_tensor("psKN", [128, 512], F32)
    nc.psum_base = pb0
    ring = [nc.alloc_psum_tensor(f"ring{i}", [128, 1024], F32) for i in range(3)]
    accS = nc.alloc_psum_tensor("accS", [128, 512], F32)
    accA = nc.alloc_psum_tensor("accA", [128, 512], F32)
    nc.psum_base = pb0
    psD = [nc.alloc_psum_tensor(f"psD{i}", [128, 1024], F32) for i in range(2)]
    psG = nc.alloc_psum_tensor("psG", [128, 1024], F32)
    psU = nc.alloc_psum_tensor("psU", [128, 1024], F32)
    assert nc.psum_base - pb0 <= 8

    plan = Plan()
    add = plan.add
    ndbg = [0]

    def dbg_sem():
        ndbg[0] += 1
        return f"d_dbg{ndbg[0]}"

    ring_tok = [None, None, None]
    ring_i = [0]

    def ring_acquire():
        idx = ring_i[0] % 3
        ring_i[0] += 1
        return idx, ring_tok[idx]

    ppair_tok = [None, None]
    accS_tok = [None]
    accA_tok = [None]

    t_ones = add("sy", lambda e: e.dma_start(ones_sb[:, :], onesD.ap()), inc="d_ones", amt=16)
    t_ident = add("sy", lambda e: e.dma_start(ident[:, :], identD.ap()), inc="d_ident", amt=16)
    t_wkv = add("gp", lambda e: e.dma_start(wkv[:, :, :], watt_r[:, :, 2048:2304]),
                inc="d_wkv", amt=16)
    t_xbf = add("gp", lambda e: e.dma_start(xbf[:, :, :], xT_r[:, :, 0:C]),
                inc="d_xbf", amt=16)
    t_tabQ1 = add("sy", lambda e: e.dma_start(tabQc[:, :], tabC.ap()[:, 0:C]),
                  inc="d_tabqc", amt=16)
    t_tabQ2 = add("sy", lambda e: e.dma_start(tabQs[:, :], tabS.ap()[:, 0:C]),
                  inc="d_tabqs", amt=16)
    t_wqh = [add("gp", lambda e, i=i: e.dma_start(wqh[i][:, :, :],
                                                  watt_r[:, :, i * 128:(i + 1) * 128]),
                 inc=f"d_wqh{i}", amt=16) for i in range(2)]

    wqh_tok = [None, None]
    negM_rd = [None, None, None]
    qp_rd = [None, None, None]
    qsq_rd = [None]
    xbf_rd = [None]
    ybf_rd = [None]
    aoT_rd = [None]
    tabQ_rd = [None]
    g_rd = [None]
    wdg_rd = [None]
    acc_zrd = [None]
    zdma_tok = [None]
    psD_tok = [None, None]
    psG_tok = [None, None]
    psU_tok = [None, None]
    rope_rd = [None]
    ln_state1 = {"prev_rstd": None, "prev_apply": None}
    ln_state2 = {"prev_rstd": None, "prev_apply": None}
    qp_toks = {}
    negM_tok = {}
    last_ao = [None]
    last_evac = [None]
    ybf_ready = [None]
    silu_tok = [None, None]
    gmul_tok = [None, None]

    def partA(h, cc):
        wb = h % 2
        qb = h % 3
        ridx, rtok = ring_acquire()

        def qfill(e, r=ridx, b=wb):
            for s2 in range(2):
                for c in range(NC16):
                    i = e.matmul(ring[r][:, s2 * 512:(s2 + 1) * 512],
                                 wqh[b][:, c, :], xbf[:, c, s2 * 512:(s2 + 1) * 512],
                                 start=(c == 0), stop=(c == NC16 - 1))
            return i
        t_qf = add("pe", qfill, waits=[t_wqh[wb], t_xbf, rtok], inc="pe")
        wqh_tok[wb] = t_qf
        xbf_rd[0] = t_qf
        if h + 2 < H:
            t_wqh[(h + 2) % 2] = add(
                "gp", lambda e, hh=h + 2: e.dma_start(
                    wqh[hh % 2][:, :, :], watt_r[:, :, hh * 128:(hh + 1) * 128]),
                waits=[wqh_tok[(h + 2) % 2]], inc=f"d_wqh{(h + 2) % 2}", amt=16)

        toks = []
        last_t1 = None
        for s2 in range(2):
            sc_ = slice(s2 * 512, (s2 + 1) * 512)
            t_cp1 = add("dve", lambda e, r=ridx, s=sc_: e.tensor_copy(rotb[0:64, :],
                                                                      ring[r][64:128, s]),
                        waits=[t_qf, rope_rd[0]])
            t_cp2 = add("dve", lambda e, r=ridx, s=sc_: e.tensor_copy(rotb[64:128, :],
                                                                      ring[r][0:64, s]))
            t_t1 = add("dve", lambda e, r=ridx, s=sc_: e.tensor_mul(
                t1b[:, :], ring[r][:, s], tabQc[:, s]),
                waits=[t_qf, t_tabQ1, rope_rd[0]])
            t_t2 = add("dve", lambda e, s=sc_: e.tensor_mul(t2b[:, :], rotb[:, :],
                                                            tabQs[:, s]),
                       waits=[t_tabQ2])
            t_qp = add("dve", lambda e, b=qb, s=sc_: e.tensor_add(Qp[b][:, s], t1b[:, :],
                                                                  t2b[:, :]),
                       waits=[qp_rd[qb]])
            toks.append(t_qp)
            rope_rd[0] = t_qp
            last_t1 = t_t1
        ring_tok[ridx] = last_t1
        tabQ_rd[0] = toks[-1]
        qp_toks[h] = toks

    def partA_sq(h):
        hb = h % 3
        t_sq = add("act", lambda e, b=hb: e.activation(qsq[:, :], Qp[b][:, :], AF.Square),
                   waits=qp_toks[h][:2] + [qsq_rd[0]])
        qp_toks[h].append(t_sq)

    def partB(h):
        hb = h % 3
        r2idx, r2tok = ring_acquire()
        t_sq = qp_toks[h][2]

        def qnfill(e, r=r2idx):
            e.matmul(ring[r][:, 0:512], ones_sb[:, :], qsq[:, 0:512], start=True, stop=True)
            return e.matmul(ring[r][:, 512:1024], ones_sb[:, :], qsq[:, 512:1024],
                            start=True, stop=True)
        t_qn = add("pe", qnfill, waits=[t_sq, t_ones, r2tok], inc="pe")
        qsq_rd[0] = t_qn
        t_qm = add("dve", lambda e, r=r2idx: e.reduce_max(qm[:, :], ring[r][:, :], axis=AX),
                   waits=[t_qn])
        ring_tok[r2idx] = t_qm
        negM_tok[h] = add("dve", lambda e, b=hb: e.tensor_scalar(
            negM[b][:, :], qm[:, :], kmax2[:, :], -0.5, ALU.add, ALU.mult),
            waits=[negM_rd[hb], kmax_tok])

    def attention(h, s2):
        hb = h % 3
        if True:
            scq = slice(s2 * 512, (s2 + 1) * 512)
            t_sum = t_av = None
            exp_q = []

            def emit_sumav(jp, t_exp):
                nonlocal t_sum, t_av

                def savf(e, q=2 * (jp % 2), a=2 * jp, b=2 * jp + 1,
                         first=(jp == 0), last=(jp == NTK // 2 - 1)):
                    e.matmul(accA[:, :], Vtok[:, a, :], P_ring[:, q, :],
                             start=first, stop=False)
                    e.matmul(accA[:, :], Vtok[:, b, :], P_ring[:, q + 1, :],
                             start=False, stop=last)
                    e.matmul(accS[:, :], ones_sb[:, :], P_ring[:, q, :],
                             start=first, stop=False)
                    return e.matmul(accS[:, :], ones_sb[:, :], P_ring[:, q + 1, :],
                                    start=False, stop=last)
                t_av = add("pe", savf,
                           waits=[t_exp, vtok_done,
                                  accS_tok[0] if jp == 0 else None,
                                  accA_tok[0] if jp == 0 else None], inc="pe")
                t_sum = t_av
                ppair_tok[jp % 2] = t_av

            for jp in range(NTK // 2):
                sidx, stok = ring_acquire()
                j0, j1 = 2 * jp, 2 * jp + 1

                def sfill(e, r=sidx, a=j0, b=j1, bb=hb, s=scq):
                    e.matmul(ring[r][:, 0:512], KT[:, a * 128:(a + 1) * 128], Qp[bb][:, s],
                             start=True, stop=True)
                    return e.matmul(ring[r][:, 512:1024], KT[:, b * 128:(b + 1) * 128],
                                    Qp[bb][:, s], start=True, stop=True)
                t_S = add("pe", sfill, waits=[stok, qp_toks[h][s2], kt_done], inc="pe")
                qp_rd[hb] = t_S

                ps = 2 * (jp % 2)
                t_exp = add("act", lambda e, r=sidx, q=ps, b=hb: e.activation(
                    P_ring[:, q:q + 2, :],
                    ring[r][:, :].rearrange("p (j t) -> p j t", j=2),
                    AF.Exp, bias=negM[b][:, :]),
                    waits=[t_S, negM_tok[h], ppair_tok[jp % 2]])
                ring_tok[sidx] = t_exp
                negM_rd[hb] = t_exp
                exp_q.append(t_exp)
                if jp >= 1:
                    emit_sumav(jp - 1, exp_q[jp - 1])
            emit_sumav(NTK // 2 - 1, exp_q[-1])

            t_rs = add("dve", lambda e: e.reciprocal(rsum[:, :], accS[:, :]),
                       waits=[t_sum])
            accS_tok[0] = t_rs
            t_ao = add("dve", lambda e, hh=h, s=scq: e.tensor_mul(
                aoT[:, hh, s], accA[:, :], rsum[:, :]),
                waits=[t_av, aoT_rd[0]])
            accA_tok[0] = t_ao
            last_ao[0] = t_ao

    def bc2(t):
        return bass.AP(t, 0, [[256, 128], [0, 2], [1, 256]])


    NP1 = T // 512
    xp_tok = None
    xp_rd = [None]
    tk_tok = [None, None]
    tk_rd = [None, None]
    psK_rd = [None, None]
    psV_rd = [None, None]
    psTR_rd = None
    psKN_rd = None
    rope1_rd = None
    vt_rd = None
    ksq_rd = None
    kt_done = None
    vtok_done = None
    kmax_tok = None

    prologue0_hook = [lambda: (partA(0, 0), partA_sq(0))]
    for p in range(NP1):
        if p == min(4, NP1 - 1) and prologue0_hook:
            prologue0_hook.pop()()
        pbuf = p % 2
        cols = slice(p * 512, (p + 1) * 512)
        xp_tok = add("gp", lambda e, c=cols: e.dma_start(
            xbf_p[:, :, :], xT_r[:, :, c]),
            waits=[xp_rd[0]], inc="d_xbfp", amt=16)
        tk_tok[0] = add("sy", lambda e, c=cols: e.dma_start(
            tkc[0][:, :], tabC.ap()[:, c]), waits=[tk_rd[0]], inc="d_tkc0", amt=16)
        tk_tok2 = add("sy", lambda e, c=cols: e.dma_start(
            tks[0][:, :], tabS.ap()[:, c]), waits=[tk_rd[0]], inc="d_tks0", amt=16)

        def kfill(e, b=pbuf):
            for c in range(NC16):
                i = e.matmul(psK[b][:, :], wkv[:, c, 0:128], xbf_p[:, c, :],
                             start=(c == 0), stop=(c == NC16 - 1))
            return i
        t_kf = add("pe", kfill, waits=[xp_tok, t_wkv, psK_rd[pbuf]], inc="pe")

        def vfill(e, b=pbuf):
            for c in range(NC16):
                i = e.matmul(psV[b][:, :], wkv[:, c, 128:256], xbf_p[:, c, :],
                             start=(c == 0), stop=(c == NC16 - 1))
            return i
        t_vf = add("pe", vfill, waits=[xp_tok, t_wkv, psV_rd[pbuf]], inc="pe")
        xp_rd[0] = t_vf

        t_cp1 = add("act", lambda e, b=pbuf: e.copy(rotb1[0:64, :], psK[b][64:128, :]),
                    waits=[t_kf, rope1_rd])
        t_cp2 = add("act", lambda e, b=pbuf: e.copy(rotb1[64:128, :], psK[b][0:64, :]))
        t_t1 = add("dve", lambda e, b=pbuf: e.tensor_mul(t1b1[:, :], psK[b][:, :],
                                                         tkc[0][:, :]),
                   waits=[t_kf, t_cp2, tk_tok[0], rope1_rd])
        psK_rd[pbuf] = [t_t1, t_cp2]
        t_t2 = add("dve", lambda e: e.tensor_mul(t2b1[:, :], rotb1[:, :],
                                                 tks[0][:, :]),
                   waits=[tk_tok2])
        tk_rd[0] = t_t2
        t_kt = add("dve", lambda e, c=cols: e.tensor_add(KT[:, c], t1b1[:, :], t2b1[:, :]))
        rope1_rd = t_kt
        kt_done = t_kt

        t_vt = add("act", lambda e, b=pbuf: e.copy(vt1[:, :], psV[b][:, :]),
                   waits=[t_vf, vt_rd])
        psV_rd[pbuf] = t_vt

        def trfill(e):
            for q in range(4):
                i = e.matmul(psTR[:, q * 128:(q + 1) * 128], vt1[:, q * 128:(q + 1) * 128],
                             ident[:, :], start=True, stop=True)
            return i
        t_tr = add("pe", trfill, waits=[t_vt, t_ident, psTR_rd], inc="pe")
        vt_rd = t_tr
        t_ev = add("act", lambda e, q=p: e.copy(
            Vtok[:, q * 4:(q + 1) * 4, :],
            psTR[:, :].rearrange("p (j h) -> p j h", j=4)), waits=[t_tr])
        psTR_rd = t_ev
        vtok_done = t_ev

        t_ks = add("act", lambda e, c=cols: e.activation(ksq1[:, :], KT[:, c], AF.Square),
                   waits=[t_kt, ksq_rd])
        t_kn = add("pe", lambda e: e.matmul(psKN[:, :], ones_sb[:, :], ksq1[:, :],
                                            start=True, stop=True),
                   waits=[t_ks, t_ones, psKN_rd], inc="pe")
        ksq_rd = t_kn
        t_red = add("dve", lambda e: e.reduce_max(knt[:, :], psKN[:, :], axis=AX),
                    waits=[t_kn])
        psKN_rd = t_red
        if p == 0:
            kmax_tok = add("dve", lambda e: e.tensor_copy(kmax2[:, :], knt[:, :]))
        else:
            kmax_tok = add("dve", lambda e: e.tensor_max(kmax2[:, :], kmax2[:, :], knt[:, :]))

    wo_prep = None
    for q in range(NC16):
        wo_prep = add("gp", lambda e, q=q: e.dma_start(
            woutB.ap()[q].rearrange("p (c o) -> p c o", c=NC16),
            wout_r[:, :, q * 128:(q + 1) * 128]), inc="d_woprep", amt=16)

    ring_tok[1] = _flat([ring_tok[1], psK_rd[0], psK_rd[1]])
    ring_tok[2] = _flat([ring_tok[2], [psV_rd[0], psV_rd[1]]])
    accS_tok[0] = _flat([accS_tok[0], [psTR_rd]])
    accA_tok[0] = _flat([accA_tok[0], [psKN_rd]])

    if dbg:
        add("sc", lambda e: e.dma_start(dbg_outs["dKT"].ap(), KT[:, :]),
            waits=[kt_done], inc=dbg_sem(), amt=16)
        add("sc", lambda e: e.dma_start(
            dbg_outs["dVtok"].ap(), Vtok[:, :, :].rearrange("p j h -> p (j h)")),
            waits=[vtok_done], inc=dbg_sem(), amt=16)
        add("sc", lambda e: e.dma_start(dbg_outs["dkmax"].ap(), kmax2[:, :]),
            waits=[kmax_tok], inc=dbg_sem(), amt=16)

    def ln_stats(src4, rbf_, rsq_, stats_waits, rb_rd):
        t_mm = None
        for g in range(4):
            gb = g % 2
            t_rb = add("act", lambda e, g_=g, b=gb: e.copy(rbf_[b][:, :, :], src4(g_)),
                       waits=stats_waits + [rb_rd[gb]])
            t_rq = add("act", lambda e, b=gb: e.activation(rsq_[b][:, :, :],
                                                           rbf_[b][:, :, :], AF.Square))

            def statmm(e, b=gb, first=(g == 0), last=(g == 3)):
                for j in range(4):
                    e.matmul(accS[:, 0:256], ones_sb[:, :], rbf_[b][:, j, :],
                             start=(first and j == 0), stop=(last and j == 3))
                    i = e.matmul(accA[:, 0:256], ones_sb[:, :], rsq_[b][:, j, :],
                                 start=(first and j == 0), stop=(last and j == 3))
                return i
            t_mm = add("pe", statmm,
                       waits=[t_rq, accS_tok[0] if g == 0 else None,
                              accA_tok[0] if g == 0 else None], inc="pe")
            rb_rd[gb] = t_mm
        return t_mm

    def ln_chain(t_mm, A, B, Cc, Dd, state):
        t_A = add("dve", lambda e: e.tensor_scalar_mul(A[:, :], accS[:, 0:256], 1.0 / D),
                  waits=[t_mm, state["prev_rstd"]])
        t_B = add("dve", lambda e: e.tensor_scalar_mul(B[:, :], accA[:, 0:256], 1.0 / D))
        accS_tok[0] = t_A
        accA_tok[0] = t_B
        add("dve", lambda e: e.tensor_mul(Cc[:, :], A[:, :], A[:, :]))
        t_var = add("dve", lambda e: e.scalar_tensor_tensor(
            Dd[:, :], B[:, :], float(EPS), Cc[:, :], ALU.add, ALU.subtract))
        t_ln = add("act", lambda e: e.activation(B[:, :], Dd[:, :], AF.Ln),
                   waits=[t_var])
        t_rstd = add("act", lambda e: e.activation(Cc[:, :], B[:, :], AF.Exp, scale=-0.5))
        state["prev_rstd"] = t_rstd
        add("dve", lambda e: e.tensor_mul(Dd[:, :], A[:, :], Cc[:, :]), waits=[t_rstd])
        return t_rstd

    def ln_apply(src2, apply_out2, Cc, Dd, U, t_rstd, apply_waits, extra4):
        out_toks = []
        for g in range(8):
            add("dve", lambda e, g_=g: e.tensor_mul(U[:, :, :], src2(g_), bc2(Cc)),
                waits=[t_rstd] + apply_waits)
            t_y = add("dve", lambda e, g_=g: e.tensor_sub(apply_out2(g_), U[:, :, :],
                                                          bc2(Dd)))
            if g % 2 == 1:
                ee = extra4(g // 2, t_y)
                out_toks.append(ee if ee is not None else t_y)
        return out_toks

    rb1_state = [None, None]
    rb2_state = [None, None]

    n_out = 0
    prologue_state = [1, 0]
    for cc in range(NCH):
        pa, pb = prologue_state
        for h in range(H):
            if h == 0 and h >= pb:
                partB(h)
            if h + 1 < H and h + 1 >= pa:
                partA(h + 1, cc)
            attention(h, 0)
            if h + 1 < H and h + 1 >= pa:
                partA_sq(h + 1)
            if h + 1 < H and h + 1 >= pb:
                partB(h + 1)
            attention(h, 1)

        wo_tok = [None, None]
        xf_tok = [None, None]
        xf_rd = [last_ao[0], last_ao[0]]
        last_pj = [None]
        ybf_half = [None, None]
        evac_of_piece = [None, None]

        def proj_piece(piece):
            pc = slice(piece * 512, (piece + 1) * 512)
            gpc0 = cc * C + piece * 512
            for e_t in range(NC16):
                eb = e_t % 2
                wo_tok[eb] = add("sy", lambda e, et=e_t: e.dma_start(
                    wqh[et % 2][:, :, :],
                    woutB.ap()[et].rearrange("p (c o) -> p c o", c=NC16)),
                    waits=[wqh_tok[eb], wo_prep], inc=f"d_wo{eb}", amt=16)
                xb = e_t % 2
                xf_tok[xb] = add("sc", lambda e, et=e_t, g=slice(gpc0, gpc0 + 512),
                                 b=xb: e.dma_start(
                    xf[b][:, :], xT_r[:, et, g]), waits=[xf_rd[xb]], inc=f"d_xf{xb}", amt=16)
                ridx, rtok = ring_acquire()

                def pjfill(e, r=ridx, b=eb, p=pc):
                    for c in range(NC16):
                        i = e.matmul(ring[r][:, 0:512], wqh[b][:, c, :], aoT[:, c, p],
                                     start=(c == 0), stop=(c == NC16 - 1))
                    return i
                t_pj = add("pe", pjfill, waits=[wo_tok[eb], last_ao[0], rtok], inc="pe")
                wqh_tok[eb] = t_pj
                aoT_rd[0] = t_pj
                last_pj[0] = t_pj
                t_ev = add("dve", lambda e, r=ridx, et=e_t, p=pc, b=xb: e.tensor_add(
                    acc[:, et, p], ring[r][:, 0:512], xf[b][:, :]),
                    waits=[t_pj, xf_tok[xb], acc_zrd[0]])
                ring_tok[ridx] = t_ev
                xf_rd[xb] = t_ev
                evac_of_piece[piece] = t_ev

        def ln1_stats(piece, sub):
            ppc = slice(piece * 512 + sub * 256, piece * 512 + sub * 256 + 256)
            return ln_stats(lambda g_, s=ppc: acc[:, 4 * g_:4 * g_ + 4, s],
                            rbf, rsq, [evac_of_piece[piece]], rb1_state)

        def ln1_finish(piece, sub, t_mm, pre):
            ppc = slice(piece * 512 + sub * 256, piece * 512 + sub * 256 + 256)

            def extra1(g_, t_y, s=ppc):
                return add("act", lambda e, g2=g_, s=s: e.copy(
                    ybf[:, 4 * g2:4 * g2 + 4, s], acc[:, 4 * g2:4 * g2 + 4, s]),
                    waits=[t_y, ybf_rd[0]])
            t_rstd = ln_chain(t_mm, lnA, lnB, lnC, lnD, ln_state1)
            t_next = pre() if pre is not None else None
            toks = ln_apply(
                lambda g_, s=ppc: acc[:, 2 * g_:2 * g_ + 2, s],
                lambda g_, s=ppc: acc[:, 2 * g_:2 * g_ + 2, s],
                lnC, lnD, lnU, t_rstd, [], extra1)
            return toks[-1], t_next

        proj_piece(0)
        t00 = ln1_stats(0, 0)
        proj_piece(1)
        _, t01 = ln1_finish(0, 0, t00, lambda: ln1_stats(0, 1))
        ybf_half[0], t10 = ln1_finish(0, 1, t01, lambda: ln1_stats(1, 0))
        _, t11 = ln1_finish(1, 0, t10, lambda: ln1_stats(1, 1))
        ybf_half[1], _ = ln1_finish(1, 1, t11, None)
        ybf_ready[0] = ybf_half[1]

        dbg_ao_tok = None
        if dbg and cc == 0:
            dbg_ao_tok = add("sc", lambda e: e.dma_start(
                dbg_outs["daoT"].ap(), aoT[:, :, :].rearrange("p c t -> p (c t)")),
                waits=[aoT_rd[0], last_ao[0]], inc=dbg_sem(), amt=16)
            dbg_y_tok = add("sc", lambda e: e.dma_start(
                dbg_outs["dy"].ap(), acc[:, :, :].rearrange("p c t -> p (c t)")),
                waits=[ybf_ready[0]], inc=dbg_sem(), amt=16)
            ybf_ready[0] = [ybf_ready[0], dbg_y_tok]

        if cc + 1 < NCH:
            nch = slice((cc + 1) * C, (cc + 2) * C)
            t_xbf = add("gp", lambda e, s=nch: e.dma_start(xbf[:, :, :], xT_r[:, :, s]),
                        waits=[xbf_rd[0]], inc="d_xbf", amt=16)
            t_tabQ1 = add("sy", lambda e, s=nch: e.dma_start(tabQc[:, :], tabC.ap()[:, s]),
                          waits=[tabQ_rd[0]], inc="d_tabqc", amt=16)
            t_tabQ2 = add("sy", lambda e, s=nch: e.dma_start(tabQs[:, :], tabS.ap()[:, s]),
                          waits=[tabQ_rd[0]], inc="d_tabqs", amt=16)
            for i in range(2):
                t_wqh[i] = add("gp", lambda e, i=i: e.dma_start(
                    wqh[i][:, :, :], watt_r[:, :, i * 128:(i + 1) * 128]),
                    waits=[wqh_tok[i]], inc=f"d_wqh{i}", amt=16)

        ffn_war = [last_pj[0], zdma_tok[0], acc_zrd[0], dbg_ao_tok]
        psG_tok[0] = _flat([psG_tok[0], ring_tok[2]])
        psG_tok[1] = _flat([psG_tok[1], ring_tok[2]])
        psU_tok[0] = _flat([psU_tok[0], [accS_tok[0], accA_tok[0]]])
        psU_tok[1] = _flat([psU_tok[1], [accS_tok[0], accA_tok[0]]])
        psD_tok[0] = _flat([psD_tok[0], ring_tok[0]])
        psD_tok[1] = _flat([psD_tok[1], ring_tok[1]])
        wg_tok = [None, None]
        wu_tok = [None, None]
        wg_rd = [ffn_war, ffn_war]
        wu_rd = [ffn_war, ffn_war]
        wdg_rd[0] = _flat([wdg_rd[0], ffn_war])
        for fg in range(FT // GSZ):
            wd_toks = []
            for f4 in range(GSZ):
                ftg = fg * GSZ + f4
                fb = ftg % 2
                wg_tok[fb] = add("gp", lambda e, f=ftg: e.dma_start(
                    wg[f % 2][:, :, :], wgT_r[:, :, f * 128:(f + 1) * 128]),
                    waits=[wg_rd[fb]], inc=f"d_wg{fb}", amt=16)
                wu_tok[fb] = add("gp", lambda e, f=ftg: e.dma_start(
                    wu[f % 2][:, :, :], wuT_r[:, :, f * 128:(f + 1) * 128]),
                    waits=[wu_rd[fb]], inc=f"d_wu{fb}", amt=16)
                wd_toks.append(add("gp", lambda e, f=ftg, q=f4: e.dma_start(
                    wdg[:, q, :], wdT_r[:, f, :]), waits=[wdg_rd[0]], inc="d_wdg", amt=16))

                t_gate = t_up = None
                for s2 in range(2):
                    hcols = slice(s2 * 512, (s2 + 1) * 512)

                    def gatef(e, b=fb, s=hcols):
                        for c in range(NC16):
                            i = e.matmul(psG[:, s], wg[b][:, c, :], ybf[:, c, s],
                                         start=(c == 0), stop=(c == NC16 - 1))
                        return i
                    t_gate = add("pe", gatef,
                                 waits=[wg_tok[fb], ybf_half[s2], psG_tok[s2]], inc="pe")
                    wg_rd[fb] = t_gate
                    ybf_rd[0] = t_gate
                    silu_tok[s2] = add("act", lambda e, s=hcols: e.activation(
                        silu_sb[:, s], psG[:, s], AF.Silu),
                        waits=[t_gate, gmul_tok[s2]])
                    psG_tok[s2] = silu_tok[s2]

                    def upf(e, b=fb, s=hcols):
                        for c in range(NC16):
                            i = e.matmul(psU[:, s], wu[b][:, c, :], ybf[:, c, s],
                                         start=(c == 0), stop=(c == NC16 - 1))
                        return i
                    t_up = add("pe", upf, waits=[wu_tok[fb], ybf_half[s2], psU_tok[s2]],
                               inc="pe")
                    wu_rd[fb] = t_up
                    ybf_rd[0] = t_up
                    gmul_tok[s2] = add("dve", lambda e, q=f4, s=hcols: e.tensor_mul(
                        gbuf[:, q, s], silu_sb[:, s], psU[:, s]),
                        waits=[silu_tok[s2], t_up, g_rd[0]])
                    psU_tok[s2] = gmul_tok[s2]

            for tp in range(2):
                tpc = slice(tp * 512, (tp + 1) * 512)
                for ep in range(8):
                    db = (tp * 8 + ep) % 2

                    def dnf(e, b=db, p=tpc, ee=ep):
                        for e2 in range(2):
                            eg = 2 * ee + e2
                            for f4 in range(GSZ):
                                i = e.matmul(psD[b][:, e2 * 512:(e2 + 1) * 512],
                                             wdg[:, f4, eg * 128:(eg + 1) * 128],
                                             gbuf[:, f4, p],
                                             start=(f4 == 0), stop=(f4 == GSZ - 1))
                        return i
                    t_dn = add("pe", dnf, waits=[wd_toks[-1], gmul_tok[tp], psD_tok[db]],
                               inc="pe")
                    g_rd[0] = t_dn
                    wdg_rd[0] = t_dn
                    t_de = add("dve", lambda e, b=db, p=tpc, ee=ep: e.tensor_add(
                        acc[:, 2 * ee:2 * ee + 2, p],
                        acc[:, 2 * ee:2 * ee + 2, p],
                        psD[b][:, :].rearrange("p (j t) -> p j t", j=2)),
                        waits=[t_dn, ybf_ready[0]])
                    psD_tok[db] = t_de

        dbg_r2_tok = None
        if dbg and cc == 0:
            dbg_r2_tok = add("sc", lambda e: e.dma_start(
                dbg_outs["dr2"].ap(), acc[:, :, :].rearrange("p c t -> p (c t)")),
                waits=[psD_tok[0], psD_tok[1]], inc=dbg_sem(), amt=16)
            acc_zrd[0] = _flat([acc_zrd[0], [dbg_r2_tok]])

        ring_tok[0] = psD_tok[0]
        ring_tok[1] = psD_tok[1]
        ring_tok[2] = [silu_tok[0], silu_tok[1]]
        accS_tok[0] = [gmul_tok[0], gmul_tok[1]]
        accA_tok[0] = [gmul_tok[0], gmul_tok[1]]

        dn_last = g_rd[0]
        tail_hooks = []
        if cc + 1 < NCH:
            ppair_tok[0] = _flat([ppair_tok[0], [dn_last]])
            ppair_tok[1] = _flat([ppair_tok[1], [dn_last]])
            qp_rd[0] = _flat([qp_rd[0], [dn_last]])
            qp_rd[1] = _flat([qp_rd[1], [dn_last]])
            qp_rd[2] = _flat([qp_rd[2], [dn_last]])
            rope_rd[0] = _flat([rope_rd[0], [gmul_tok[0], gmul_tok[1], dn_last]])
            qsq_rd[0] = _flat([qsq_rd[0], [dn_last]])
            partA(0, cc + 1)
            partA_sq(0)
            partB(0)
            tail_hooks = [lambda: partA(1, cc + 1),
                          lambda: (partA_sq(1), partB(1)),
                          lambda: partA(2, cc + 1),
                          lambda: (partA_sq(2), partB(2))]
            prologue_state[0], prologue_state[1] = 3, 3

        NPP = C // 256

        def ln2_src4(pp):
            s = slice(pp * 256, (pp + 1) * 256)
            return lambda g_: acc[:, 4 * g_:4 * g_ + 4, s]

        t_mm_next = ln_stats(ln2_src4(0), rbf2, rsq2,
                             [psD_tok[0], psD_tok[1], dbg_r2_tok], rb2_state)
        for pp in range(NPP):
            ppc = slice(pp * 256, (pp + 1) * 256)
            gpc = slice(cc * C + pp * 256, cc * C + (pp + 1) * 256)
            t_rstd = ln_chain(t_mm_next, lnA2, lnB2, lnC2, lnD2, ln_state2)
            if pp + 1 < NPP:
                t_mm_next = ln_stats(ln2_src4(pp + 1), rbf2, rsq2,
                                     [psD_tok[0], psD_tok[1], dbg_r2_tok], rb2_state)
            toks = ln_apply(
                lambda g_, s=ppc: acc[:, 2 * g_:2 * g_ + 2, s],
                lambda g_: z_stage[:, 2 * g_:2 * g_ + 2, :],
                lnC2, lnD2, lnU2, t_rstd, [zdma_tok[0]], lambda g_, t: None)
            acc_zrd[0] = toks[-1]
            zdma_tok[0] = add("sc", lambda e, g=gpc: e.dma_start(
                zT_r[:, :, g], z_stage[:, :, :]), waits=toks, inc="d_z", amt=16)
            n_out += 1
            if tail_hooks:
                tail_hooks.pop(0)()

        aoT_rd[0] = [dn_last, zdma_tok[0]]
        if cc + 1 >= NCH:
            ppair_tok[0] = _flat([ppair_tok[0], [dn_last]])
            ppair_tok[1] = _flat([ppair_tok[1], [dn_last]])

    plan.wait_only("sc", [zdma_tok[0]])

    sem_names = set()
    for _eng, _fn, waits, inc, _amt, _snap in plan.steps:
        if inc is not None:
            sem_names.add(inc)
        for (s, _v, _p) in waits:
            sem_names.add(s)
    sems = {nm: nc.alloc_semaphore(f"s_{nm}") for nm in sorted(sem_names)}
    with nc.Block() as blk:

        @blk.tensor
        def _(e):
            emit(plan, e, ("pe",), sems, serialize)

        @blk.scalar
        def _(e):
            emit(plan, e, ("act", "sc"), sems, serialize)

        @blk.vector
        def _(e):
            emit(plan, e, ("dve",), sems, serialize)

        @blk.sync
        def _(e):
            emit(plan, e, ("sy",), sems, serialize)

        @blk.gpsimd
        def _(e):
            emit(plan, e, ("gp",), sems, serialize)

    return nc


def make_tables(T=4096):
    hd = 128
    a = np.arange(0, hd, 2, dtype=np.float32)
    pos = np.float32(10000.0) ** ((-2.0 * a - 1.0) / np.float32(hd))
    t = np.arange(T, dtype=np.float32)
    angles = t[:, None] * pos[None, :]
    rot = np.concatenate([angles, angles], axis=-1)
    s4 = np.float32(128.0) ** np.float32(-0.25)
    cosT = (np.cos(rot).astype(np.float32).T * s4).astype(np.float32)
    sinM = (np.sin(rot).astype(np.float32).T * s4).astype(np.float32)
    sinM[:64] *= -1.0
    return cosT.astype(ml_dtypes.bfloat16), sinM.astype(ml_dtypes.bfloat16)


def make_in_maps(x, watt, wout, wgate, wup, wdown, T=4096, n_cores=8):
    watt = np.ascontiguousarray(np.asarray(watt, np.float32))
    wout = np.ascontiguousarray(np.asarray(wout, np.float32))
    wgT = np.ascontiguousarray(np.asarray(wgate, np.float32).T)
    wuT = np.ascontiguousarray(np.asarray(wup, np.float32).T)
    wdT = np.ascontiguousarray(np.asarray(wdown, np.float32).T)
    tabC, tabS = make_tables(T)
    ident = np.eye(128, dtype=ml_dtypes.bfloat16)
    ones = np.ones((128, 128), dtype=ml_dtypes.bfloat16)
    return [{
        "xT": np.ascontiguousarray(np.asarray(x[b], np.float32).T),
        "watt": watt, "wout": wout, "wgT": wgT, "wuT": wuT, "wdT": wdT,
        "tabC": tabC, "tabS": tabS, "identD": ident, "onesD": ones,
    } for b in range(n_cores)]


def kernel(L0_attn_x, L0_attn_w_attn, L0_attn_w_out, L0_ffn_w_gate, L0_ffn_w_up,
           L0_ffn_w_down):
    T = 4096
    in_maps = make_in_maps(L0_attn_x, L0_attn_w_attn, L0_attn_w_out,
                           L0_ffn_w_gate, L0_ffn_w_up, L0_ffn_w_down, T=T)
    nc = build(T=T)
    res = run_bass_kernel_spmd(nc, in_maps, core_ids=list(range(8)))
    out = np.empty((8, T, 2048), np.float32)
    for b in range(8):
        out[b] = res.results[b]["zT"].T
    return out

